# version 10
# baseline (speedup 1.0000x reference)
import math
from contextlib import ExitStack

import numpy as np

import concourse.bass as bass
import concourse.mybir as mybir
from concourse.alu_op_type import AluOpType
from concourse.alu_op_type import AluOpType as OP
from concourse.bass_utils import run_bass_kernel_spmd



F32 = mybir.dt.float32
BF16 = mybir.dt.bfloat16
AF = mybir.ActivationFunctionType

ENGINES = ("pe", "dve", "act", "pool", "sp")
Q_ENGINE = {"qsp": "sp", "qact": "act", "qpool": "pool"}


def _key_of(ap_or_handle):
    t = getattr(ap_or_handle, "tensor", ap_or_handle)
    return t.name


class Prog:
    def __init__(self, nc):
        self.nc = nc
        self._ctx = []
        self.sem = {}
        for name in ENGINES:
            self.sem[name] = self._enter(nc.semaphore(f"sem_{name}"))
        self.cnt = {n: 0 for n in ENGINES}
        self.slot_sem = {}
        self.slot_cnt = {}
        self.hwm = {}
        self.ops = {n: [] for n in ENGINES}
        self.last_write = {}
        self.readers = {}
        self.n_waits = 0
        self.n_ops = 0

    def _enter(self, cm):
        v = cm.__enter__()
        self._ctx.append(cm)
        return v

    def close(self):
        for cm in reversed(self._ctx):
            cm.__exit__(None, None, None)
        self._ctx = []

    def _slot(self, name):
        if name not in self.slot_sem:
            self.slot_sem[name] = self._enter(self.nc.semaphore(f"sem_{name}"))
            self.slot_cnt[name] = 0
        return self.slot_sem[name]

    def _wait(self, eng, src, val):
        if val <= 0:
            return
        if src == eng == "pe":
            return
        k = (eng, src)
        if self.hwm.get(k, 0) >= val:
            return
        self.hwm[k] = val
        sem = self.sem[src] if src in ENGINES else self.slot_sem[src]
        self.ops[eng].append(("w", lambda e, sem=sem, val=val: e.wait_ge(sem, val)))
        self.n_waits += 1

    def _read_deps(self, eng, keys):
        for k in keys:
            lw = self.last_write.get(k)
            if lw is not None:
                self._wait(eng, lw[0], lw[1])

    def _write_deps(self, eng, keys):
        for k in keys:
            lw = self.last_write.get(k)
            if lw is not None:
                self._wait(eng, lw[0], lw[1])
            for src, v in self.readers.get(k, {}).items():
                self._wait(eng, src, v)

    def _record_read(self, src, cnt, keys):
        for k in keys:
            self.readers.setdefault(k, {})
            prev = self.readers[k].get(src, 0)
            self.readers[k][src] = max(prev, cnt)

    def _record_write(self, src, cnt, keys):
        for k in keys:
            self.last_write[k] = (src, cnt)
            self.readers[k] = {}

    def op(self, eng, fn, outs, ins, inc_extra=0, name=None):
        okeys = [x if isinstance(x, str) else _key_of(x) for x in outs]
        ikeys = [x if isinstance(x, str) else _key_of(x) for x in ins]
        self._read_deps(eng, ikeys)
        self._write_deps(eng, okeys)
        self.cnt[eng] += 1
        cnt = self.cnt[eng]
        sem = self.sem[eng]

        def run(e, fn=fn, sem=sem):
            inst = fn(e)
            inst.then_inc(sem, 1)
        self.ops[eng].append(("o", run))
        self.n_ops += 1
        self._record_read(eng, cnt, ikeys)
        self._record_write(eng, cnt, okeys)
        return cnt

    def group(self, eng, fns, outs, ins):
        okeys = [x if isinstance(x, str) else _key_of(x) for x in outs]
        ikeys = [x if isinstance(x, str) else _key_of(x) for x in ins]
        self._read_deps(eng, ikeys)
        self._write_deps(eng, okeys)
        self.cnt[eng] += 1
        cnt = self.cnt[eng]
        sem = self.sem[eng]

        def run(e, fns=fns, sem=sem):
            inst = None
            for f in fns:
                inst = f(e)
            inst.then_inc(sem, 1)
        self.ops[eng].append(("o", run))
        self.n_ops += len(fns)
        self._record_read(eng, cnt, ikeys)
        self._record_write(eng, cnt, okeys)
        return cnt

    def dma(self, queue, dst, src, slot=None, dst_key=None, src_key=None):
        eng = Q_ENGINE[queue]
        dk = [dst_key or _key_of(dst)]
        sk = [src_key or _key_of(src)]
        slot = slot or dk[0]
        sem = self._slot(slot)
        self._read_deps(eng, sk)
        self._write_deps(eng, dk)
        self.slot_cnt[slot] += 16
        cnt = self.slot_cnt[slot]

        def run(e, dst=dst, src=src, sem=sem):
            e.dma_start(dst, src).then_inc(sem, 16)
        self.ops[eng].append(("o", run))
        self.n_ops += 1
        self._record_read(slot, cnt, sk)
        self._record_write(slot, cnt, dk)
        return cnt

    def barrier_wait_all_dma(self):
        for s, c in self.slot_cnt.items():
            if c:
                self._wait("sp", s, c)

    def fence(self):
        srcs = ([(e, self.cnt[e]) for e in ENGINES] +
                [(s, c) for s, c in self.slot_cnt.items()])
        for eng in ENGINES:
            for src, val in srcs:
                if val > 0:
                    self._wait(eng, src, val)

    def emit(self):
        nc = self.nc
        with nc.Block() as blk:
            for eng, attr in (("sp", "sync"), ("pe", "tensor"), ("dve", "vector"),
                              ("act", "scalar"), ("pool", "gpsimd")):
                lst = self.ops[eng]
                if not lst:
                    continue
                def body(e, lst=lst):
                    for kind, fn in lst:
                        fn(e)
                getattr(blk, attr)(body)



def tt(p, out, a, b, op, okey=None, akey=None, bkey=None):
    p.op("dve", lambda e: e.tensor_tensor(out, a, b, op),
         [okey or out], [akey or a, bkey or b])


def tt_pool(p, out, a, b, op):
    p.op("pool", lambda e: e.tensor_tensor(out, a, b, op), [out], [a, b])


def ts(p, out, in0, s1, s2, op0, op1=AluOpType.bypass, okey=None, ikey=None):
    ins = [ikey or in0]
    for s in (s1, s2):
        if isinstance(s, bass.AP):
            ins.append(s)
    p.op("dve", lambda e: e.tensor_scalar(out, in0, s1, s2, op0, op1),
         [okey or out], ins)


def stt(p, out, in0, scalar, in1, op0, op1, okey=None):
    ins = [in0, in1]
    if isinstance(scalar, bass.AP):
        ins.append(scalar)
    p.op("dve", lambda e: e.scalar_tensor_tensor(out, in0, scalar, in1, op0, op1),
         [okey or out], ins)


def act(p, out, in_, func, bias=None, scale=None, okey=None, ikey=None,
        accum_out=None):
    kw = {}
    ins = [ikey or in_]
    if bias is not None:
        kw["bias"] = bias
        if isinstance(bias, bass.AP):
            ins.append(bias)
    if scale is not None:
        kw["scale"] = scale
        if isinstance(scale, bass.AP):
            ins.append(scale)
    outs = [okey or out]
    if accum_out is not None:
        kw["accum_out"] = accum_out
        outs.append(accum_out)
    p.op("act", lambda e: e.activation(out, in_, func, **kw), outs, ins)


def act_copy(p, out, in_, okey=None, ikey=None, scale=None, bias=None):
    act(p, out, in_, AF.Copy if bias is None else AF.Identity,
        bias=bias, scale=scale, okey=okey, ikey=ikey)


def dve_copy(p, out, in_, okey=None, ikey=None):
    p.op("dve", lambda e: e.tensor_copy(out, in_), [okey or out], [ikey or in_])


def scan(p, out, d0, d1, initial, okey=None, keys=None):
    ins = [d0, d1] if keys is None else keys
    if isinstance(initial, bass.AP):
        ins = list(ins) + [initial]
    p.op("dve", lambda e: e.tensor_tensor_scan(out, d0, d1, initial,
                                               AluOpType.mult, AluOpType.add),
         [okey or out], ins)


def mm(p, out, lhsT, rhs, start=True, stop=True, okey=None, lkey=None, rkey=None):
    p.op("pe", lambda e: e.matmul(out, lhsT, rhs, start=start, stop=stop),
         [okey or out], [lkey or lhsT, rkey or rhs])


def mm_group(p, out, pairs, okey=None, ikeys=None, start=True, stop=True):
    fns = []
    n = len(pairs)
    for i, (l, r) in enumerate(pairs):
        fns.append(lambda e, l=l, r=r, s=(start and i == 0),
                   t=(stop and i == n - 1):
                   e.matmul(out, l, r, start=s, stop=t))
    ins = ikeys if ikeys is not None else [x for pr in pairs for x in pr]
    p.group("pe", fns, [okey or out], ins)


def transpose(p, out, in_, ident, okey=None, ikey=None):
    p.op("pe", lambda e: e.transpose(out, in_, ident),
         [okey or out], [ikey or in_, ident])


def memset(p, eng, out, val, okey=None):
    p.op(eng, lambda e: e.memset(out, val), [okey or out], [])


"""Megakernel builder for mk_hybrid_d1: full forward of
  x += attn_a(LN(x)); x += ffn(LN(x)); x += ssm(LN(x)); x += ffn(LN(x))
on one NeuronCore per batch element (8 cores, data-parallel over batch).

Layouts:
  residual x: DRAM (T=2048 rows t, D=2048 cols d) fp32, streamed in 128-row tiles
  ln_dm: SBUF (128 p=d%128, 16 dblk, T) bf16  (feature-major for matmuls)
  weights: host pre-tiled to (loop_idx, partition, free...) fp32; DMA-cast bf16
  scratch (per core, Internal DRAM): q_s/k_s (dmblk,p,t) bf16, v_s (tblk,p,dv),
  hid_s (hblk,p,t), w2p_s (p,B,o) [FWHT-folded W2'], hs_s/dt_s/sg_s/y_s
  (db,p,t) bf16, xA/xB fp32, bcflat_s.
"""


T = 2048
D = 2048
NH, HD = 16, 128
FH = 8192
DI = 4096
NS = 16
RK = 128
SQRT2 = 1.4142135623730951

ALL_PHASES = ("w2p", "attn_a", "ffn1", "attn_b", "ffn2")


def _slopes(n):
    start = 2.0 ** (-2.0 ** (-(math.log2(n) - 3)))
    return np.array([start ** (i + 1) for i in range(n)])


def _had(n):
    i = np.arange(n)
    H = np.empty((n, n), np.float32)
    for a in range(n):
        H[a] = np.array([(-1.0) ** bin(a & j).count("1") for j in i], np.float32)
    return H


def stage_shared(w):
    f = np.float32
    s = {}
    A = np.ascontiguousarray
    wqkv = w["L0_attn_a_w_qkv"]
    wqkT = wqkv.T
    qk = wqkT[:, :4096].reshape(16, 128, 32, 128)
    s["wqkT"] = A(qk.transpose(2, 1, 0, 3)).astype(f)
    vv = wqkT[:, 4096:].reshape(16, 128, 4, 512)
    s["wvT"] = A(vv.transpose(2, 1, 0, 3)).astype(f)
    bqkv = w["L0_attn_a_b_qkv"]
    s["bqk_cols"] = A(bqkv[:4096].reshape(32, 128).T).astype(f)
    s["bv_rep"] = A(np.broadcast_to(bqkv[4096:], (128, 2048))).astype(f)
    s["woutT_t"] = A(w["L0_attn_a_w_out"].T.reshape(16, 128, 2048)
                     .transpose(1, 0, 2)).astype(f)
    s["bout_rep"] = A(np.broadcast_to(w["L0_attn_a_b_out"], (128, 2048))).astype(f)
    w1T = w["L0_ffn_W1"].T
    s["w1T_t"] = A(w1T.reshape(16, 128, 64, 128).transpose(2, 1, 0, 3)).astype(f)
    w3T = w["L0_ffn_W3"].T
    s["w3T_t"] = A(w3T.reshape(16, 128, 64, 128).transpose(2, 1, 0, 3)).astype(f)
    w2T = w["L0_ffn_W2"].T
    s["w2T_t"] = A(w2T.reshape(64, 128, 16, 128).transpose(2, 1, 0, 3)).astype(f)
    s["winsT_t"] = A(w["L0_attn_b_W_in_states"].T.reshape(16, 128, 32, 128)
                     .transpose(2, 1, 0, 3)).astype(f)
    s["wingT_t"] = A(w["L0_attn_b_W_in_gates"].T.reshape(16, 128, 32, 128)
                     .transpose(2, 1, 0, 3)).astype(f)
    s["convw_cols"] = A(w["L0_attn_b_conv_w"].reshape(32, 128, 4)
                        .transpose(1, 0, 2).reshape(128, 128)).astype(f)
    s["convb_cols"] = A(w["L0_attn_b_conv_b"].reshape(32, 128).T).astype(f)
    s["wdtT_t"] = A(w["L0_attn_b_W_dt"].T.reshape(32, 128, 128)
                    .transpose(1, 0, 2)).astype(f)
    s["wdtpT_t"] = A(w["L0_attn_b_W_dtproj"].T.reshape(128, 32, 128)).astype(f)
    s["bdtp_cols"] = A(w["L0_attn_b_b_dtproj"].reshape(32, 128).T).astype(f)
    wbc = np.concatenate([w["L0_attn_b_W_b"], w["L0_attn_b_W_c"]], 0)
    s["wbcT_t"] = A(wbc.T.reshape(32, 128, 32).transpose(1, 0, 2)).astype(f)
    s["alog_cols"] = A(w["L0_attn_b_A_log"].reshape(32, 128, 16)
                       .transpose(1, 0, 2).reshape(128, 512)).astype(f)
    s["d_cols"] = A(w["L0_attn_b_D"].reshape(32, 128).T).astype(f)
    s["woutbT_t"] = A(w["L0_attn_b_W_out"].T.reshape(32, 128, 2048)
                      .transpose(1, 0, 2)).astype(f)
    lam = np.exp(-_slopes(NH))
    pos = np.arange(128, dtype=np.float64)
    dq = lam[:, None] ** (pos + 1.0)
    dk = lam[:, None] ** (127.0 - pos)
    diff = pos[:, None] - pos[None, :]
    dm = np.where(diff >= 0, lam[:, None, None] ** np.maximum(diff, 0), 0.0)
    s["dq_rep"] = A(np.broadcast_to(dq[None], (128, NH, 128))).astype(f)
    s["maskT"] = A(dm.transpose(2, 0, 1)).astype(f)
    s["dk_cols"] = A(dk.T).astype(f)
    s["h128"] = _had(128) * (2.0 ** -7)
    s["ident"] = np.eye(128, dtype=f)
    s["ones1"] = np.ones((1, 128), f)
    return s


def lam_c_floats():
    lam = np.exp(-_slopes(NH))
    return [float(x) for x in lam ** 128]


class C:
    def __init__(self, nc, p, stack, io):
        self.nc, self.p, self.stack, self.io = nc, p, stack, io
        self.sb_cache = {}

    def sb(self, name, shape, dt=F32):
        t = self.stack.enter_context(self.nc.sbuf_tensor(name, list(shape), dt))
        return t

    def ps(self, name, shape, dt=F32):
        return self.stack.enter_context(self.nc.psum_tensor(name, list(shape), dt))

    def dram(self, name, shape, dt, kind="Internal"):
        return self.nc.dram_tensor(name, list(shape), dt, kind=kind)


def build_init(c):
    p, io = c.p, c.io
    for name, shape, dt, src in [
        ("ident_f", (128, 128), F32, "ident"),
        ("ident_b", (128, 128), BF16, "ident"),
        ("h128_b", (128, 128), BF16, "h128"),
        ("bqk_cols", (128, 32), F32, "bqk_cols"),
        ("bv_rep", (128, 2048), BF16, "bv_rep"),
        ("bout_rep", (128, 2048), BF16, "bout_rep"),
        ("dq_rep", (128, NH, 128), BF16, "dq_rep"),
        ("maskT", (128, NH, 128), BF16, "maskT"),
        ("dk_cols", (128, NH), F32, "dk_cols"),
        ("convw_cols", (128, 128), F32, "convw_cols"),
        ("convb_cols", (128, 32), F32, "convb_cols"),
        ("bdtp_cols", (128, 32), F32, "bdtp_cols"),
        ("alog_cols", (128, 512), F32, "alog_cols"),
        ("d_cols", (128, 32), F32, "d_cols"),
    ]:
        t = c.sb("s_" + name, shape, dt)
        setattr(c, name, t)
        p.dma("qsp" if dt == F32 else "qpool", t[:], io[src])
    c.ones1 = c.sb("s_ones1", (1, 128), F32)
    p.dma("qsp", c.ones1[:], io["ones1"])
    c.c_eps = c.sb("c_eps", (128, 1), F32)
    memset(p, "pool", c.c_eps[:], 1e-5)
    c.c_one = c.sb("c_one", (128, 1), F32)
    memset(p, "pool", c.c_one[:], 1.0)
    c.aneg = c.sb("aneg", (128, 512), F32)
    act(p, c.aneg[:], c.alog_cols[:], AF.Exp)
    ts(p, c.aneg[:], c.aneg[:], -1.0, None, OP.mult)
    c.psA = c.ps("psA", (128, 2048), F32)
    c.psB = c.ps("psB", (128, 1024), F32)
    c.ptr = [c.ps("ptr0", (128, 128), BF16), c.ps("ptr1", (128, 128), BF16)]
    c.q_s = c.dram("q_s", (16, 128, T), BF16)
    c.k_s = c.dram("k_s", (16, 128, T), BF16)
    c.v_s = c.dram("v_s", (16, 128, D), BF16)
    c.hid_s = c.dram("hid_s", (64, 128, T), BF16)
    c.w2p_s = c.dram("w2p_s", (128, 64, 2048), BF16)
    c.hs_s = c.dram("hs_s", (32, 128, T), BF16)
    c.dt_s = c.dram("dt_s", (32, 128, T), BF16)
    c.sg_s = c.dram("sg_s", (32, 128, T), BF16)
    c.y_s = c.dram("y_s", (32, 128, T), BF16)
    c.bcflat_s = c.dram("bcflat_s", (2, 16, T), F32)
    c.xA = c.dram("xA", (T, D), F32)
    c.xB = c.dram("xB", (T, D), F32)


def psA(c, i, w=512):
    return c.psA[:, i * 512:i * 512 + w], f"psA{i}"


def psB(c, i, w=512):
    return c.psB[:, i * 512:i * 512 + w], f"psB{i}"


def gen_w2p(c, st):
    p, io = c.p, c.io
    OW = 128
    if True:
        wch = [st.enter_context(c.nc.sbuf_tensor("w2c0", [128, 64, OW], BF16))]
        fa = st.enter_context(c.nc.sbuf_tensor("w2fa", [128, 64, OW], F32))
        fb = st.enter_context(c.nc.sbuf_tensor("w2fb", [128, 64, OW], F32))
        outb = st.enter_context(c.nc.sbuf_tensor("w2ob", [128, 64, OW], BF16))
        for och in range(16):
            wc = wch[0]
            p.dma("qpool", wc[:], io["w2T_t"][och], slot="w2c0")
            for B in range(64):
                pa = c.psA[:, (2 + B % 2) * 512:(2 + B % 2) * 512 + OW]
                key = f"hb{2 + B % 2}"
                mm(p, pa, c.h128_b[:], wc[:, B, :], okey=key,
                   lkey="h128_b", rkey="w2c0")
                act(p, fa[:, B, :], pa, AF.Copy, scale=SQRT2,
                    okey="w2fa", ikey=key)
            bufs = [fa, fb]
            cur = 0
            for si, h in enumerate([1, 2, 4, 8, 16, 32]):
                src, dst = bufs[cur], bufs[1 - cur]
                last = si == 5
                if last:
                    dst = outb
                sap = src[:].rearrange("p (g x j) o -> p g x j o", x=2, j=h)
                dap = dst[:].rearrange("p (g x j) o -> p g x j o", x=2, j=h)
                tt(p, dap[:, :, 0], sap[:, :, 0], sap[:, :, 1], OP.add,
                   okey=dst.name, akey=src.name, bkey=src.name)
                tt(p, dap[:, :, 1], sap[:, :, 0], sap[:, :, 1], OP.subtract,
                   okey=dst.name, akey=src.name, bkey=src.name)
                cur = 1 - cur
            for bb in range(0, 64, 8):
                p.dma("qact", c.w2p_s.ap()[:, bb:bb + 8, och * OW:(och + 1) * OW],
                      outb[:, bb:bb + 8, :], slot="w2p_s", dst_key="w2p_s",
                      src_key="w2ob")
            yield


def build_w2p(c):
    with ExitStack() as st:
        for _ in gen_w2p(c, st):
            pass
        c.p.fence()


def emit_ln_dm(c, x_src, ln_dm, tag):
    p = c.p
    with ExitStack() as st:
        xt = [st.enter_context(c.nc.sbuf_tensor(f"lnx{tag}{i}", [128, D], F32))
              for i in range(2)]
        lnt = [st.enter_context(c.nc.sbuf_tensor(f"lnt{tag}{i}", [128, D], BF16))
               for i in range(2)]
        st6 = [st.enter_context(c.nc.sbuf_tensor(f"st6{tag}{i}", [128, 4, 6], F32))
               for i in range(2)]
        st2 = [st.enter_context(c.nc.sbuf_tensor(f"st2{tag}{i}", [128, 2], F32))
               for i in range(2)]
        ngm = [st.enter_context(c.nc.sbuf_tensor(f"ngm{tag}{i}", [128, 1], F32))
               for i in range(2)]
        rsd = [st.enter_context(c.nc.sbuf_tensor(f"rsd{tag}{i}", [128, 1], F32))
               for i in range(2)]
        p.dma("qsp", xt[0][:], x_src[0:128, :], slot=f"lnx{tag}0")
        for tb in range(16):
            i = tb % 2
            if tb < 15:
                ii = (tb + 1) % 2
                p.dma("qsp", xt[ii][:],
                      x_src[(tb + 1) * 128:(tb + 2) * 128, :],
                      slot=f"lnx{tag}{ii}")
            for k in range(4):
                p.op("dve", lambda e, a=st6[i], b=xt[i], k=k:
                     e.bn_stats(a[:, k, :], b[:, k * 512:(k + 1) * 512]),
                     [st6[i]], [xt[i]])
            p.op("dve", lambda e, a=st2[i], b=st6[i]:
                 e.bn_aggr(a[:], b[:].rearrange("p a b -> p (a b)")),
                 [st2[i]], [st6[i]])
            ts(p, ngm[i][:], st2[i][:, 0:1], -1.0, None, OP.mult,
               okey=ngm[i].name, ikey=st2[i].name)
            act(p, rsd[i][:], st2[i][:, 1:2], AF.Sqrt, bias=c.c_eps[:],
                okey=rsd[i].name, ikey=st2[i].name)
            p.op("dve", lambda e, a=rsd[i]: e.reciprocal(a[:], a[:]),
                 [rsd[i]], [rsd[i]])
            ts(p, lnt[i][:], xt[i][:], ngm[i][:], rsd[i][:], OP.add, OP.mult)
            for db in range(16):
                tr = c.ptr[db % 2]
                transpose(p, tr[:], lnt[i][:, db * 128:(db + 1) * 128],
                          c.ident_b[:], okey=tr.name, ikey=lnt[i].name)
                act_copy(p, ln_dm[:, db, tb * 128:(tb + 1) * 128], tr[:],
                         okey=f"{ln_dm.name}.{tb // 4}", ikey=tr.name)
        p.fence()


def build_attn_a(c, x_src, x_dst, x_dst_name, with_w2p=False):
    p, io = c.p, c.io
    lam_c = lam_c_floats()
    with ExitStack() as st:
        ln_dm = st.enter_context(c.nc.sbuf_tensor("ln_a", [128, 16, T], BF16))
        emit_ln_dm(c, x_src, ln_dm, "a")
        with ExitStack() as st2:
            w2st = ExitStack()
            w2gen = gen_w2p(c, w2st) if with_w2p else None
            wqk = [st2.enter_context(c.nc.sbuf_tensor(f"wqk{i}", [128, 16, 128],
                                                      BF16)) for i in range(2)]
            qkt = [st2.enter_context(c.nc.sbuf_tensor(f"qkt{i}", [128, 512],
                                                      BF16)) for i in range(2)]
            p.dma("qpool", wqk[0][:], io["wqkT"][0], slot=wqk[0].name)
            for ms in range(32):
                if w2gen is not None:
                    next(w2gen, None)
                if ms < 31:
                    wn = wqk[(ms + 1) % 2]
                    p.dma("qpool", wn[:], io["wqkT"][ms + 1], slot=wn.name)
                wb = wqk[ms % 2]
                dst = c.q_s if ms < 16 else c.k_s
                blk = ms % 16
                for tn in range(4):
                    j4 = (ms * 4 + tn) % 4
                    j = (ms * 4 + tn) % 2
                    pb, key = psB(c, j4) if j4 < 2 else psA(c, j4 - 2)
                    mm_group(p, pb, [(wb[:, kb, :],
                                      ln_dm[:, kb, tn * 512:(tn + 1) * 512])
                                     for kb in range(16)],
                             okey=key, ikeys=[wb.name, f"ln_a.{tn}"])
                    act(p, qkt[j][:], pb, AF.Identity,
                        bias=c.bqk_cols[:, ms:ms + 1], okey=qkt[j].name, ikey=key)
                    p.dma("qact", dst.ap()[blk, :, tn * 512:(tn + 1) * 512],
                          qkt[j][:], slot=dst.name, dst_key=dst.name,
                          src_key=qkt[j].name)
            if w2gen is not None:
                for _ in w2gen:
                    pass
                w2st.close()
                p.fence()
            wv = [st2.enter_context(c.nc.sbuf_tensor(f"wv{i}", [128, 16, 512],
                                                     BF16)) for i in range(2)]
            vt = [st2.enter_context(c.nc.sbuf_tensor(f"vt{i}", [128, 512], BF16))
                  for i in range(2)]
            p.dma("qpool", wv[0][:], io["wvT"][0], slot=wv[0].name)
            for dv in range(4):
                if dv < 3:
                    wn = wv[(dv + 1) % 2]
                    p.dma("qpool", wn[:], io["wvT"][dv + 1], slot=wn.name)
                wb = wv[dv % 2]
                for tsl in range(16):
                    j4 = (dv * 16 + tsl) % 4
                    j = (dv * 16 + tsl) % 2
                    pb, key = psB(c, j4) if j4 < 2 else psA(c, j4 - 2)
                    mm_group(p, pb, [(ln_dm[:, kb, tsl * 128:(tsl + 1) * 128],
                                      wb[:, kb, :]) for kb in range(16)],
                             okey=key, ikeys=[wb.name, f"ln_a.{tsl // 4}"])
                    tt(p, vt[j][:], pb, c.bv_rep[:, dv * 512:(dv + 1) * 512],
                       OP.add, okey=vt[j].name, akey=key, bkey="bv_rep")
                    p.dma("qact", c.v_s.ap()[tsl, :, dv * 512:(dv + 1) * 512],
                          vt[j][:], slot="v_s", dst_key="v_s",
                          src_key=vt[j].name)
            p.fence()
    with ExitStack() as st:
        nsb = c.nc.sbuf_tensor
        wout = st.enter_context(nsb("wout_r", [128, 16, 2048], BF16))
        for i in range(4):
            p.dma("qpool", wout[:, i * 4:(i + 1) * 4, :],
                  io["woutT_t"][:, i * 4:(i + 1) * 4, :], slot="wout_r",
                  dst_key="wout_r")
        qc = st.enter_context(nsb("qc", [128, 16, 512], BF16))
        kc = st.enter_context(nsb("kc", [128, 16, 512], BF16))
        vc = st.enter_context(nsb("vc", [128, 4, 2048], BF16))
        S_f = st.enter_context(nsb("S_f", [128, NH, 128], F32))
        S_b = st.enter_context(nsb("S_b", [128, NH, 128], BF16))
        Stmp = [st.enter_context(nsb(f"Stmp{i}", [128, 128], F32)) for i in range(4)]
        smask = [st.enter_context(nsb(f"smask{i}", [128, 128], BF16)) for i in range(4)]
        ktdk = [st.enter_context(nsb(f"ktdk{i}", [128, 128], BF16)) for i in range(4)]
        qdq = [st.enter_context(nsb(f"qdq{i}", [128, 128], BF16)) for i in range(4)]
        attn = st.enter_context(nsb("attn_dm", [128, 16, 512], BF16))
        xt = st.enter_context(nsb("xta", [128, 2048], F32))
        xnew = st.enter_context(nsb("xna", [128, 2048], F32))
        p.op("dve", lambda e: e.memset(S_f[:], 0.0),
             [f"S_f.{h}" for h in range(NH)], [])
        p.op("dve", lambda e: e.memset(S_b[:], 0.0),
             [f"S_b.{h}" for h in range(NH)], [])
        for tch in range(4):
            sl = slice(tch * 512, (tch + 1) * 512)
            for g in range(4):
                p.dma("qsp", qc[:, g * 4:(g + 1) * 4, :],
                      c.q_s.ap()[g * 4:(g + 1) * 4, :, sl]
                      .rearrange("b p t -> p b t"), slot="qc", dst_key="qc",
                      src_key="q_s")
                p.dma("qsp", kc[:, g * 4:(g + 1) * 4, :],
                      c.k_s.ap()[g * 4:(g + 1) * 4, :, sl]
                      .rearrange("b p t -> p b t"), slot="kc", dst_key="kc",
                      src_key="k_s")
            for i in range(4):
                p.dma("qsp", vc[:, i, :], c.v_s.ap()[tch * 4 + i],
                      slot="vc", dst_key="vc", src_key="v_s")
            for lc in range(4):
                cs = slice(lc * 128, (lc + 1) * 128)
                def hbank(b):
                    if b < 4:
                        return c.psA[:, b * 512:b * 512 + 128], f"hb{b}"
                    return c.psB[:, (b - 4) * 512:(b - 4) * 512 + 128], f"hb{b}"

                def stage1(h):
                    b = h % 6
                    pbk, key = hbank(b)
                    mm(p, pbk, kc[:, h, cs], qc[:, h, cs], okey=key,
                       lkey="kc", rkey="qc")
                    tr = c.ptr[h % 2]
                    transpose(p, tr[:], kc[:, h, cs], c.ident_b[:],
                              okey=f"ptr{h % 2}", ikey="kc")

                def stage2(h):
                    b = h % 6
                    pbk, key = hbank(b)
                    tt(p, smask[h % 4][:], pbk, c.maskT[:, h, :], OP.mult,
                       okey=smask[h % 4].name, akey=key, bkey="maskT")
                    ts(p, ktdk[h % 4][:], c.ptr[h % 2][:],
                       c.dk_cols[:, h:h + 1], None, OP.mult,
                       okey=ktdk[h % 4].name, ikey=f"ptr{h % 2}")
                    tt(p, qdq[h % 4][:], qc[:, h, cs], c.dq_rep[:, h, :],
                       OP.mult, okey=qdq[h % 4].name, akey="qc", bkey="dq_rep")

                def stage3(h):
                    b = h % 6
                    pbk, key = hbank(b)
                    v_t = vc[:, lc, h * 128:(h + 1) * 128]
                    mm(p, pbk, S_b[:, h, :], qdq[h % 4][:],
                       start=True, stop=False, okey=key, lkey=f"S_b.{h}",
                       rkey=qdq[h % 4].name)
                    mm(p, pbk, v_t, smask[h % 4][:], start=False, stop=True,
                       okey=key, lkey="vc", rkey=smask[h % 4].name)

                def stage4(h):
                    b = h % 6
                    pbk, key = hbank(b)
                    act_copy(p, attn[:, h, cs], pbk, okey=f"attn.{lc}",
                             ikey=key)

                def stage5(h):
                    b = h % 6
                    pbk, key = hbank(b)
                    v_t = vc[:, lc, h * 128:(h + 1) * 128]
                    mm(p, pbk, ktdk[h % 4][:], v_t, okey=key,
                       lkey=ktdk[h % 4].name, rkey="vc")
                    ts(p, Stmp[h % 4][:], S_f[:, h, :], lam_c[h], None,
                       OP.mult, okey=Stmp[h % 4].name, ikey=f"S_f.{h}")
                    tt(p, S_f[:, h, :], Stmp[h % 4][:], pbk, OP.add,
                       okey=f"S_f.{h}", akey=Stmp[h % 4].name, bkey=key)
                    act_copy(p, S_b[:, h, :], S_f[:, h, :], okey=f"S_b.{h}",
                             ikey=f"S_f.{h}")

                stage1(0)
                stage2(0)
                for h in range(NH):
                    if h + 1 < NH:
                        stage1(h + 1)
                        stage2(h + 1)
                    stage3(h)
                    stage4(h)
                    stage5(h)
            for tsl in range(4):
                row = tch * 512 + tsl * 128
                p.dma("qsp", xt[:], x_src[row:row + 128, :], slot="xta")
                for oc in range(4):
                    if oc % 2 == 0:
                        pb = c.psB[:, 512:1024]
                        keys = ["hb5"]
                    else:
                        pb = c.psA[:, 1536:2048]
                        keys = ["hb3"]
                    fns = []
                    for db in range(16):
                        fns.append(lambda e, o=pb, l=attn[:, db, tsl * 128:
                                                         (tsl + 1) * 128],
                                   r=wout[:, db, oc * 512:(oc + 1) * 512],
                                   s=(db == 0), t=(db == 15):
                                   e.matmul(o, l, r, start=s, stop=t))
                    p.group("pe", fns, keys, [f"attn.{tsl}", "wout_r"])
                    osl = slice(oc * 512, (oc + 1) * 512)
                    tt(p, xnew[:, osl], pb, c.bout_rep[:, osl], OP.add,
                       okey="xna", akey=keys[0], bkey="bout_rep")
                    tt(p, xnew[:, osl], xnew[:, osl], xt[:, osl], OP.add,
                       okey="xna", akey="xna", bkey="xta")
                p.dma("qact", x_dst[row:row + 128, :], xnew[:],
                      slot=x_dst_name, dst_key=x_dst_name, src_key="xna")
        p.fence()


def build_ffn(c, x_src, x_dst, x_dst_name, tag):
    p, io = c.p, c.io
    with ExitStack() as st:
        nsb = c.nc.sbuf_tensor
        ln_dm = st.enter_context(nsb(f"ln_f{tag}", [128, 16, T], BF16))
        emit_ln_dm(c, x_src, ln_dm, f"f{tag}")
        with ExitStack() as st2:
            w1b = [st2.enter_context(nsb(f"w1b{tag}{i}", [128, 16, 128], BF16))
                   for i in range(2)]
            w3b = [st2.enter_context(nsb(f"w3b{tag}{i}", [128, 16, 128], BF16))
                   for i in range(2)]
            gt = [st2.enter_context(nsb(f"gt{tag}{i}", [128, 512], F32))
                  for i in range(2)]
            ht = [st2.enter_context(nsb(f"ht{tag}{i}", [128, 512], BF16))
                  for i in range(2)]
            p.dma("qpool", w1b[0][:], io["w1T_t"][0], slot=w1b[0].name)
            p.dma("qpool", w3b[0][:], io["w3T_t"][0], slot=w3b[0].name)
            for hs_ in range(64):
                i = hs_ % 2
                if hs_ < 63:
                    ii = (hs_ + 1) % 2
                    p.dma("qpool", w1b[ii][:], io["w1T_t"][hs_ + 1],
                          slot=w1b[ii].name)
                    p.dma("qpool", w3b[ii][:], io["w3T_t"][hs_ + 1],
                          slot=w3b[ii].name)
                for tn in range(4):
                    tsl = slice(tn * 512, (tn + 1) * 512)
                    j = tn % 2
                    g_ps, g_key = psA(c, j)
                    l_ps, l_key = psA(c, 2 + j)
                    mm_group(p, g_ps, [(w1b[i][:, kb, :], ln_dm[:, kb, tsl])
                                       for kb in range(16)],
                             okey=g_key, ikeys=[w1b[i].name, f"{ln_dm.name}.{tn}"])
                    mm_group(p, l_ps, [(w3b[i][:, kb, :], ln_dm[:, kb, tsl])
                                       for kb in range(16)],
                             okey=l_key, ikeys=[w3b[i].name, f"{ln_dm.name}.{tn}"])
                    act(p, gt[j][:], g_ps, AF.Silu, okey=gt[j].name, ikey=g_key)
                    tt(p, ht[j][:], gt[j][:], l_ps, OP.mult, okey=ht[j].name,
                       akey=gt[j].name, bkey=l_key)
                    p.dma("qact", c.hid_s.ap()[hs_, :, tsl], ht[j][:],
                          slot="hid_s", dst_key="hid_s", src_key=ht[j].name)
            p.fence()
    with ExitStack() as st:
        nsb = c.nc.sbuf_tensor
        w2po = st.enter_context(nsb(f"w2po{tag}", [128, 64, 512], BF16))
        hidb = [st.enter_context(nsb(f"hidb{tag}{i}", [128, 64, 256], BF16))
                for i in range(2)]
        xoc = [st.enter_context(nsb(f"xoc{tag}{i}", [128, 512], F32)) for i in range(2)]
        xpt = [st.enter_context(nsb(f"xpt{tag}{i}", [128, 512], F32)) for i in range(2)]
        for oc in range(4):
            for bb in range(0, 64, 8):
                p.dma("qsp", w2po[:, bb:bb + 8, :],
                      c.w2p_s.ap()[:, bb:bb + 8, oc * 512:(oc + 1) * 512],
                      slot=w2po.name, dst_key=w2po.name, src_key="w2p_s")
            for tch in range(8):
                i = tch % 2
                for bb in range(0, 64, 16):
                    p.dma("qsp", hidb[i][:, bb:bb + 16, :],
                          c.hid_s.ap()[bb:bb + 16, :,
                                       tch * 256:(tch + 1) * 256]
                          .rearrange("b p t -> p b t"),
                          slot=hidb[i].name, dst_key=hidb[i].name,
                          src_key="hid_s")
                for tsl in range(2):
                    j = (tch * 2 + tsl) % 2
                    row = tch * 256 + tsl * 128
                    p.dma("qsp", xoc[j][:], x_src[row:row + 128,
                                                  oc * 512:(oc + 1) * 512],
                          slot=xoc[j].name)
                    pb, key = psB(c, j)
                    mm_group(p, pb, [(hidb[i][:, hb, tsl * 128:(tsl + 1) * 128],
                                      w2po[:, hb, :]) for hb in range(64)],
                             okey=key, ikeys=[hidb[i].name, w2po.name])
                    tt(p, xpt[j][:], pb, xoc[j][:], OP.add, okey=xpt[j].name,
                       akey=key, bkey=xoc[j].name)
                    p.dma("qact", x_dst[row:row + 128, oc * 512:(oc + 1) * 512],
                          xpt[j][:], slot=x_dst_name, dst_key=x_dst_name,
                          src_key=xpt[j].name)
        p.fence()


def build_attn_b(c, x_src, x_dst, x_dst_name):
    p, io = c.p, c.io
    with ExitStack() as st:
        nsb = c.nc.sbuf_tensor
        ln_dm = st.enter_context(nsb("ln_b", [128, 16, T], BF16))
        emit_ln_dm(c, x_src, ln_dm, "b")
        with ExitStack() as st2:
            wdt = st2.enter_context(nsb("wdt_r", [128, 32, 128], BF16))
            p.dma("qpool", wdt[:], io["wdtT_t"], slot="wdt_r")
            wdtp = st2.enter_context(nsb("wdtp_r", [128, 32, 128], BF16))
            p.dma("qpool", wdtp[:], io["wdtpT_t"], slot="wdtp_r")
            wins = [st2.enter_context(nsb(f"wins{i}", [128, 16, 128], BF16))
                    for i in range(2)]
            wing = [st2.enter_context(nsb(f"wing{i}", [128, 16, 128], BF16))
                    for i in range(2)]
            hs0 = [st2.enter_context(nsb(f"hs0_{i}", [128, 2051], F32))
                   for i in range(2)]
            cv = [st2.enter_context(nsb(f"cv{i}", [128, 2048], F32))
                  for i in range(2)]
            hsb = [st2.enter_context(nsb(f"hsb{i}", [128, 2048], BF16))
                   for i in range(2)]
            sgb = [st2.enter_context(nsb(f"sgb{i}", [128, 2048], BF16))
                   for i in range(2)]
            memset(p, "dve", hs0[0][:, 0:3], 0.0)
            memset(p, "dve", hs0[1][:, 0:3], 0.0)
            p.dma("qpool", wins[0][:], io["winsT_t"][0], slot=wins[0].name)
            p.dma("qpool", wing[0][:], io["wingT_t"][0], slot=wing[0].name)
            for db in range(32):
                i = db % 2
                if db < 31:
                    ii = (db + 1) % 2
                    p.dma("qpool", wins[ii][:], io["winsT_t"][db + 1],
                          slot=wins[ii].name)
                    p.dma("qpool", wing[ii][:], io["wingT_t"][db + 1],
                          slot=wing[ii].name)
                for tn in range(4):
                    j = tn % 2
                    tsl = slice(tn * 512, (tn + 1) * 512)
                    pb, key = psB(c, j)
                    mm_group(p, pb, [(wins[i][:, kb, :], ln_dm[:, kb, tsl])
                                     for kb in range(16)],
                             okey=key, ikeys=[wins[i].name, f"ln_b.{tn}"])
                    act_copy(p, hs0[i][:, 3 + tn * 512: 3 + (tn + 1) * 512],
                             pb, okey=hs0[i].name, ikey=key)
                ts(p, cv[0][:], hs0[i][:, 0:2048],
                   c.convw_cols[:, db * 4:db * 4 + 1],
                   c.convb_cols[:, db:db + 1], OP.mult, OP.add,
                   okey="cv0", ikey=hs0[i].name)
                stt(p, cv[1][:], hs0[i][:, 1:2049],
                    c.convw_cols[:, db * 4 + 1:db * 4 + 2], cv[0][:],
                    OP.mult, OP.add, okey="cv1")
                stt(p, cv[0][:], hs0[i][:, 2:2050],
                    c.convw_cols[:, db * 4 + 2:db * 4 + 3], cv[1][:],
                    OP.mult, OP.add, okey="cv0")
                stt(p, cv[1][:], hs0[i][:, 3:2051],
                    c.convw_cols[:, db * 4 + 3:db * 4 + 4], cv[0][:],
                    OP.mult, OP.add, okey="cv1")
                act(p, hsb[i][:], cv[1][:], AF.Silu, okey=hsb[i].name, ikey="cv1")
                p.dma("qact", c.hs_s.ap()[db], hsb[i][:], slot="hs_s",
                      dst_key="hs_s", src_key=hsb[i].name)
                for tn in range(4):
                    j = tn % 2
                    tsl = slice(tn * 512, (tn + 1) * 512)
                    pb, key = psB(c, j)
                    mm_group(p, pb, [(wing[i][:, kb, :], ln_dm[:, kb, tsl])
                                     for kb in range(16)],
                             okey=key, ikeys=[wing[i].name, f"ln_b.{tn}"])
                    act(p, sgb[i][:, tsl], pb, AF.Silu, okey=sgb[i].name,
                        ikey=key)
                p.dma("qact", c.sg_s.ap()[db], sgb[i][:], slot="sg_s",
                      dst_key="sg_s", src_key=sgb[i].name)
                for tn in range(4):
                    pa, key = psA(c, tn)
                    mm(p, pa, wdt[:, db, :],
                       hsb[i][:, tn * 512:(tn + 1) * 512],
                       start=(db == 0), stop=(db == 31),
                       okey=key, lkey="wdt_r", rkey=hsb[i].name)
            tsbf = st2.enter_context(nsb("tsbf", [128, 2048], BF16))
            for tn in range(4):
                pa, key = psA(c, tn)
                act_copy(p, tsbf[:, tn * 512:(tn + 1) * 512], pa,
                         okey="tsbf", ikey=key)
            et = [st2.enter_context(nsb(f"et{i}", [128, 512], F32))
                  for i in range(2)]
            dtb = [st2.enter_context(nsb(f"dtb{i}", [128, 512], BF16))
                   for i in range(2)]
            for db in range(32):
                for tn in range(4):
                    j = (db * 4 + tn) % 2
                    pb, key = psB(c, j)
                    mm(p, pb, wdtp[:, db, :], tsbf[:, tn * 512:(tn + 1) * 512],
                       okey=key, lkey="wdtp_r", rkey="tsbf")
                    act(p, et[j][:], pb, AF.Exp, bias=c.bdtp_cols[:, db:db + 1],
                        okey=et[j].name, ikey=key)
                    act(p, dtb[j][:], et[j][:], AF.Ln, bias=c.c_one[:],
                        okey=dtb[j].name, ikey=et[j].name)
                    p.dma("qact", c.dt_s.ap()[db, :, tn * 512:(tn + 1) * 512],
                          dtb[j][:], slot="dt_s", dst_key="dt_s",
                          src_key=dtb[j].name)
            p.fence()
    nsb = c.nc.sbuf_tensor
    with ExitStack() as st2:
        wbc = st2.enter_context(nsb("wbc_r", [128, 32, 32], BF16))
        p.dma("qpool", wbc[:], io["wbcT_t"], slot="wbc_r")
        hsb2 = [st2.enter_context(nsb(f"hsc{i}", [128, 2048], BF16))
                for i in range(2)]
        p.dma("qsp", hsb2[0][:], c.hs_s.ap()[0], slot=hsb2[0].name)
        for db in range(32):
            i = db % 2
            if db < 31:
                ii = (db + 1) % 2
                p.dma("qsp", hsb2[ii][:], c.hs_s.ap()[db + 1],
                      slot=hsb2[ii].name)
            for tsl in range(16):
                mm(p, c.psB[:, tsl * 32:(tsl + 1) * 32],
                   hsb2[i][:, tsl * 128:(tsl + 1) * 128], wbc[:, db, :],
                   start=(db == 0), stop=(db == 31),
                   okey="psB0", lkey=hsb2[i].name, rkey="wbc_r")
        bc_sb = st2.enter_context(nsb("bc_sb", [128, 512], F32))
        act_copy(p, bc_sb[:], c.psB[:, 0:512], okey="bc_sb", ikey="psB0")
        bcT = st2.enter_context(nsb("bcT", [32, 16, 128], F32))
        for tsl in range(16):
            transpose(p, c.psB[0:32, 512:640], bc_sb[:, tsl * 32:(tsl + 1) * 32],
                      c.ident_f[:], okey="psB1", ikey="bc_sb")
            act_copy(p, bcT[:, tsl, :], c.psB[0:32, 512:640],
                     okey="bcT", ikey="psB1")
        p.dma("qact", c.bcflat_s.ap().rearrange("b n t -> (b n) t"),
              bcT[:].rearrange("p a b -> p (a b)"), slot="bcflat_s",
              dst_key="bcflat_s", src_key="bcT")
        p.fence()
    with ExitStack() as st2:
        B_rep = st2.enter_context(nsb("B_rep", [128, NS, 1024], BF16))
        C_rep = st2.enter_context(nsb("C_rep", [128, NS, 1024], BF16))
        dAv = st2.enter_context(nsb("dAv", [128, 2, 1024], F32))
        dBu = [st2.enter_context(nsb(f"dBu{i}", [128, NS, 1024], BF16))
               for i in range(2)]
        hv = st2.enter_context(nsb("hv", [128, 2, 1024], BF16))
        carry = st2.enter_context(nsb("carry", [128, 512], F32))
        dtb2 = [st2.enter_context(nsb(f"dtc{i}", [128, 1024], BF16))
                for i in range(2)]
        hsb3 = [st2.enter_context(nsb(f"hsd{i}", [128, 1024], BF16))
                for i in range(2)]
        sgb2 = [st2.enter_context(nsb(f"sgc{i}", [128, 1024], BF16))
                for i in range(2)]
        dtu = [st2.enter_context(nsb(f"dtu{i}", [128, 1024], BF16))
               for i in range(2)]
        t1 = st2.enter_context(nsb("t1_s", [128, 1024], F32))
        ybf = [st2.enter_context(nsb(f"ybf{i}", [128, 1024], BF16))
               for i in range(2)]
        memset(p, "dve", carry[:], 0.0)

        def stage_in(db, tch):
            i = db % 2
            tsl = slice(tch * 1024, (tch + 1) * 1024)
            p.dma("qsp", dtb2[i][:], c.dt_s.ap()[db, :, tsl],
                  slot=dtb2[i].name)
            p.dma("qsp", hsb3[i][:], c.hs_s.ap()[db, :, tsl],
                  slot=hsb3[i].name)
            p.dma("qsp", sgb2[i][:], c.sg_s.ap()[db, :, tsl],
                  slot=sgb2[i].name)
            tt(p, dtu[i][:], dtb2[i][:], hsb3[i][:], OP.mult,
               okey=dtu[i].name, akey=dtb2[i].name, bkey=hsb3[i].name)
            dtu_b = bass.AP(dtu[i], 0, [[1024, 128], [0, NS // 2], [1, 1024]])
            for half in range(2):
                hn = slice(half * 8, half * 8 + 8)
                p.op("pool", lambda e, o=dBu[i][:, hn, :],
                     a=B_rep[:, hn, :], bb=dtu_b:
                     e.tensor_tensor(o, a, bb, OP.mult),
                     [f"{dBu[i].name}.{half}"], [B_rep, dtu[i]])

        for tch in range(2):
            tsl = slice(tch * 1024, (tch + 1) * 1024)
            for bc, rep, rname in ((0, B_rep, "B_rep"), (1, C_rep, "C_rep")):
                srcap = bass.AP(c.bcflat_s, bc * NS * T + tch * 1024,
                                [[0, 128], [T, NS], [1, 1024]])
                p.dma("qpool", rep[:], srcap, slot=rname,
                      src_key="bcflat_s")
            stage_in(0, tch)
            for db in range(32):
                i = db % 2
                if db < 31:
                    stage_in(db + 1, tch)
                for n in range(NS):
                    s = n % 2
                    act(p, dAv[:, s, :], dtb2[i][:], AF.Exp,
                        scale=c.aneg[:, db * 16 + n:db * 16 + n + 1],
                        okey=f"dA{s}", ikey=dtb2[i].name)
                    scan(p, hv[:, s, :], dAv[:, s, :], dBu[i][:, n, :],
                         carry[:, db * 16 + n:db * 16 + n + 1],
                         okey=f"hv{s}",
                         keys=[f"dA{s}", f"{dBu[i].name}.{n // 8}", "carry"])
                    p.op("act", lambda e, d=carry[:, db * 16 + n:
                                                   db * 16 + n + 1],
                         sr=hv[:, s, 1023:1024]: e.copy(d, sr),
                         ["carry"], [f"hv{s}"])
                    tt(p, dBu[i][:, n, :], hv[:, s, :], C_rep[:, n, :],
                       OP.mult, okey=f"{dBu[i].name}.{n // 8}",
                       akey=f"hv{s}", bkey="C_rep")
                k0, k1 = f"{dBu[i].name}.0", f"{dBu[i].name}.1"
                p.op("dve", lambda e, o=dBu[i][:, 0:8, :], a=dBu[i][:, 0:8, :],
                     b=dBu[i][:, 8:16, :]: e.tensor_tensor(o, a, b, OP.add),
                     [k0], [k0, k1])
                p.op("dve", lambda e, o=dBu[i][:, 8:12, :], a=dBu[i][:, 0:4, :],
                     b=dBu[i][:, 4:8, :]: e.tensor_tensor(o, a, b, OP.add),
                     [k1], [k0])
                p.op("dve", lambda e, o=dBu[i][:, 12:14, :],
                     a=dBu[i][:, 8:10, :], b=dBu[i][:, 10:12, :]:
                     e.tensor_tensor(o, a, b, OP.add), [k1], [k1])
                ts(p, t1[:], hsb3[i][:], c.d_cols[:, db:db + 1], None,
                   OP.mult, okey="t1_s", ikey=hsb3[i].name)
                tt(p, t1[:], t1[:], dBu[i][:, 12, :], OP.add,
                   okey="t1_s", akey="t1_s", bkey=f"{dBu[i].name}.1")
                tt(p, t1[:], t1[:], dBu[i][:, 13, :], OP.add,
                   okey="t1_s", akey="t1_s", bkey=f"{dBu[i].name}.1")
                tt(p, ybf[i][:], t1[:], sgb2[i][:], OP.mult,
                   okey=ybf[i].name, akey="t1_s", bkey=sgb2[i].name)
                p.dma("qact", c.y_s.ap()[db, :, tsl], ybf[i][:],
                      slot="y_s", dst_key="y_s", src_key=ybf[i].name)
        p.fence()
    with ExitStack() as st:
        nsb = c.nc.sbuf_tensor
        woutb = st.enter_context(nsb("woutb_r", [128, 32, 2048], BF16))
        for i in range(8):
            p.dma("qpool", woutb[:, i * 4:(i + 1) * 4, :],
                  c.io["woutbT_t"][:, i * 4:(i + 1) * 4, :], slot="woutb_r",
                  dst_key="woutb_r")
        yb = st.enter_context(nsb("yb", [128, 32, 512], BF16))
        xt = st.enter_context(nsb("xtb", [128, 2048], F32))
        xnew = st.enter_context(nsb("xnb", [128, 2048], F32))
        for tch in range(4):
            tsl = slice(tch * 512, (tch + 1) * 512)
            for db in range(32):
                p.dma("qsp", yb[:, db, :], c.y_s.ap()[db, :, tsl],
                      slot="yb", dst_key="yb", src_key="y_s")
            for tq in range(4):
                row = tch * 512 + tq * 128
                p.dma("qsp", xt[:], x_src[row:row + 128, :], slot="xtb")
                for oc in range(4):
                    pb, key = psB(c, oc % 2)
                    mm_group(p, pb, [(yb[:, db, tq * 128:(tq + 1) * 128],
                                      woutb[:, db, oc * 512:(oc + 1) * 512])
                                     for db in range(32)],
                             okey=key, ikeys=["yb", "woutb_r"])
                    osl = slice(oc * 512, (oc + 1) * 512)
                    tt(p, xnew[:, osl], pb, xt[:, osl], OP.add,
                       okey="xnb", akey=key, bkey="xtb")
                p.dma("qact", x_dst[row:row + 128, :], xnew[:],
                      slot=x_dst_name, dst_key=x_dst_name, src_key="xnb")
        p.fence()


def build_program(phases=ALL_PHASES, x_out_taps=()):
    nc = bass.Bass("TRN2", target_bir_lowering=False)
    shared_names = list(stage_shared_dummy().keys())
    io = {}
    for name, arr in stage_shared_dummy().items():
        io[name] = nc.dram_tensor(name, list(arr.shape),
                                  mybir.dt.from_np(arr.dtype),
                                  kind="ExternalInput").ap()
    x0 = nc.dram_tensor("x0", [T, D], F32, kind="ExternalInput").ap()
    out = nc.dram_tensor("out", [T, D], F32, kind="ExternalOutput").ap()
    with ExitStack() as stack:
        p = Prog(nc)
        stack.callback(p.close)
        c = C(nc, p, stack, io)
        build_init(c)
        xA, xB = c.xA.ap(), c.xB.ap()
        if "w2p" in phases and "attn_a" not in phases:
            build_w2p(c)
        cur, cur_name = x0, "x0"
        if "attn_a" in phases:
            build_attn_a(c, cur, xA, "xA", with_w2p="w2p" in phases)
            cur, cur_name = xA, "xA"
        if "ffn1" in phases:
            dst, dname = (xB, "xB") if ("attn_b" in phases or "ffn2" in phases) \
                else (out, "out")
            build_ffn(c, cur, dst, dname, "1")
            cur, cur_name = dst, dname
        if "attn_b" in phases:
            dst, dname = (xA, "xA") if "ffn2" in phases else (out, "out")
            build_attn_b(c, cur, dst, dname)
            cur, cur_name = dst, dname
        if "ffn2" in phases:
            build_ffn(c, cur, out, "out", "2")
            cur, cur_name = out, "out"
        if cur_name != "out":
            p.dma("qsp", out, cur, slot="out", dst_key="out", src_key=cur_name)
        p.barrier_wait_all_dma()
        p.emit()
    nc.finalize()
    return nc


_DUMMY = None


def stage_shared_dummy():
    global _DUMMY
    if _DUMMY is None:
        w = {}
        rng = np.random.default_rng(0)
        w["L0_attn_a_w_qkv"] = np.zeros((6144, 2048), np.float32)
        w["L0_attn_a_b_qkv"] = np.zeros((6144,), np.float32)
        w["L0_attn_a_w_out"] = np.zeros((2048, 2048), np.float32)
        w["L0_attn_a_b_out"] = np.zeros((2048,), np.float32)
        w["L0_ffn_W1"] = np.zeros((8192, 2048), np.float32)
        w["L0_ffn_W2"] = np.zeros((2048, 8192), np.float32)
        w["L0_ffn_W3"] = np.zeros((8192, 2048), np.float32)
        w["L0_attn_b_W_in_states"] = np.zeros((4096, 2048), np.float32)
        w["L0_attn_b_W_in_gates"] = np.zeros((4096, 2048), np.float32)
        w["L0_attn_b_conv_w"] = np.zeros((4096, 4), np.float32)
        w["L0_attn_b_conv_b"] = np.zeros((4096,), np.float32)
        w["L0_attn_b_W_dt"] = np.zeros((128, 4096), np.float32)
        w["L0_attn_b_W_b"] = np.zeros((16, 4096), np.float32)
        w["L0_attn_b_W_c"] = np.zeros((16, 4096), np.float32)
        w["L0_attn_b_W_dtproj"] = np.zeros((4096, 128), np.float32)
        w["L0_attn_b_b_dtproj"] = np.zeros((4096,), np.float32)
        w["L0_attn_b_A_log"] = np.zeros((4096, 16), np.float32)
        w["L0_attn_b_D"] = np.zeros((4096,), np.float32)
        w["L0_attn_b_W_out"] = np.zeros((2048, 4096), np.float32)
        _DUMMY = stage_shared(w)
    return _DUMMY


def kernel(**inputs):
    x = np.asarray(inputs["L0_attn_a_x"])
    staged = stage_shared(inputs)
    nc = build_program(ALL_PHASES)
    in_maps = []
    for b in range(8):
        m = dict(staged)
        m["x0"] = np.ascontiguousarray(x[b], dtype=np.float32)
        in_maps.append(m)
    res = run_bass_kernel_spmd(nc, in_maps, core_ids=list(range(8)))
    out = np.stack([np.asarray(res.results[b]["out"]) for b in range(8)])
    return out.astype(np.float32, copy=False)


# version 12
# speedup vs baseline: 1.0860x; 1.0860x over previous
import math
from contextlib import ExitStack

import numpy as np

import concourse.bass as bass
import concourse.mybir as mybir
from concourse.alu_op_type import AluOpType
from concourse.alu_op_type import AluOpType as OP
from concourse.bass_utils import run_bass_kernel_spmd



F32 = mybir.dt.float32
BF16 = mybir.dt.bfloat16
AF = mybir.ActivationFunctionType

ENGINES = ("pe", "dve", "act", "pool", "sp")
Q_ENGINE = {"qsp": "sp", "qact": "act", "qpool": "pool"}


def _key_of(ap_or_handle):
    t = getattr(ap_or_handle, "tensor", ap_or_handle)
    return t.name


class Prog:
    def __init__(self, nc):
        self.nc = nc
        self._ctx = []
        self.sem = {}
        for name in ENGINES:
            self.sem[name] = self._enter(nc.semaphore(f"sem_{name}"))
        self.cnt = {n: 0 for n in ENGINES}
        self.slot_sem = {}
        self.slot_cnt = {}
        self.hwm = {}
        self.ops = {n: [] for n in ENGINES}
        self.last_write = {}
        self.readers = {}
        self.n_waits = 0
        self.n_ops = 0

    def _enter(self, cm):
        v = cm.__enter__()
        self._ctx.append(cm)
        return v

    def close(self):
        for cm in reversed(self._ctx):
            cm.__exit__(None, None, None)
        self._ctx = []

    def _slot(self, name):
        if name not in self.slot_sem:
            self.slot_sem[name] = self._enter(self.nc.semaphore(f"sem_{name}"))
            self.slot_cnt[name] = 0
        return self.slot_sem[name]

    def _wait(self, eng, src, val):
        if val <= 0:
            return
        if src == eng == "pe":
            return
        k = (eng, src)
        if self.hwm.get(k, 0) >= val:
            return
        self.hwm[k] = val
        sem = self.sem[src] if src in ENGINES else self.slot_sem[src]
        self.ops[eng].append(("w", lambda e, sem=sem, val=val: e.wait_ge(sem, val)))
        self.n_waits += 1

    def _read_deps(self, eng, keys):
        for k in keys:
            lw = self.last_write.get(k)
            if lw is not None:
                self._wait(eng, lw[0], lw[1])

    def _write_deps(self, eng, keys):
        for k in keys:
            lw = self.last_write.get(k)
            if lw is not None:
                self._wait(eng, lw[0], lw[1])
            for src, v in self.readers.get(k, {}).items():
                self._wait(eng, src, v)

    def _record_read(self, src, cnt, keys):
        for k in keys:
            self.readers.setdefault(k, {})
            prev = self.readers[k].get(src, 0)
            self.readers[k][src] = max(prev, cnt)

    def _record_write(self, src, cnt, keys):
        for k in keys:
            self.last_write[k] = (src, cnt)
            self.readers[k] = {}

    def op(self, eng, fn, outs, ins, inc_extra=0, name=None):
        okeys = [x if isinstance(x, str) else _key_of(x) for x in outs]
        ikeys = [x if isinstance(x, str) else _key_of(x) for x in ins]
        self._read_deps(eng, ikeys)
        self._write_deps(eng, okeys)
        self.cnt[eng] += 1
        cnt = self.cnt[eng]
        sem = self.sem[eng]

        def run(e, fn=fn, sem=sem):
            inst = fn(e)
            inst.then_inc(sem, 1)
        self.ops[eng].append(("o", run))
        self.n_ops += 1
        self._record_read(eng, cnt, ikeys)
        self._record_write(eng, cnt, okeys)
        return cnt

    def group(self, eng, fns, outs, ins):
        okeys = [x if isinstance(x, str) else _key_of(x) for x in outs]
        ikeys = [x if isinstance(x, str) else _key_of(x) for x in ins]
        self._read_deps(eng, ikeys)
        self._write_deps(eng, okeys)
        self.cnt[eng] += 1
        cnt = self.cnt[eng]
        sem = self.sem[eng]

        def run(e, fns=fns, sem=sem):
            inst = None
            for f in fns:
                inst = f(e)
            inst.then_inc(sem, 1)
        self.ops[eng].append(("o", run))
        self.n_ops += len(fns)
        self._record_read(eng, cnt, ikeys)
        self._record_write(eng, cnt, okeys)
        return cnt

    def dma(self, queue, dst, src, slot=None, dst_key=None, src_key=None):
        eng = Q_ENGINE[queue]
        dk = [dst_key or _key_of(dst)]
        sk = [src_key or _key_of(src)]
        slot = slot or dk[0]
        sem = self._slot(slot)
        self._read_deps(eng, sk)
        self._write_deps(eng, dk)
        self.slot_cnt[slot] += 16
        cnt = self.slot_cnt[slot]

        def run(e, dst=dst, src=src, sem=sem):
            e.dma_start(dst, src).then_inc(sem, 16)
        self.ops[eng].append(("o", run))
        self.n_ops += 1
        self._record_read(slot, cnt, sk)
        self._record_write(slot, cnt, dk)
        return cnt

    def barrier_wait_all_dma(self):
        for s, c in self.slot_cnt.items():
            if c:
                self._wait("sp", s, c)

    def fence(self):
        srcs = ([(e, self.cnt[e]) for e in ENGINES] +
                [(s, c) for s, c in self.slot_cnt.items()])
        for eng in ENGINES:
            for src, val in srcs:
                if val > 0:
                    self._wait(eng, src, val)

    def emit(self):
        nc = self.nc
        with nc.Block() as blk:
            for eng, attr in (("sp", "sync"), ("pe", "tensor"), ("dve", "vector"),
                              ("act", "scalar"), ("pool", "gpsimd")):
                lst = self.ops[eng]
                if not lst:
                    continue
                def body(e, lst=lst):
                    for kind, fn in lst:
                        fn(e)
                getattr(blk, attr)(body)



def tt(p, out, a, b, op, okey=None, akey=None, bkey=None):
    p.op("dve", lambda e: e.tensor_tensor(out, a, b, op),
         [okey or out], [akey or a, bkey or b])


def tt_pool(p, out, a, b, op):
    p.op("pool", lambda e: e.tensor_tensor(out, a, b, op), [out], [a, b])


def ts(p, out, in0, s1, s2, op0, op1=AluOpType.bypass, okey=None, ikey=None):
    ins = [ikey or in0]
    for s in (s1, s2):
        if isinstance(s, bass.AP):
            ins.append(s)
    p.op("dve", lambda e: e.tensor_scalar(out, in0, s1, s2, op0, op1),
         [okey or out], ins)


def stt(p, out, in0, scalar, in1, op0, op1, okey=None):
    ins = [in0, in1]
    if isinstance(scalar, bass.AP):
        ins.append(scalar)
    p.op("dve", lambda e: e.scalar_tensor_tensor(out, in0, scalar, in1, op0, op1),
         [okey or out], ins)


def act(p, out, in_, func, bias=None, scale=None, okey=None, ikey=None,
        accum_out=None):
    kw = {}
    ins = [ikey or in_]
    if bias is not None:
        kw["bias"] = bias
        if isinstance(bias, bass.AP):
            ins.append(bias)
    if scale is not None:
        kw["scale"] = scale
        if isinstance(scale, bass.AP):
            ins.append(scale)
    outs = [okey or out]
    if accum_out is not None:
        kw["accum_out"] = accum_out
        outs.append(accum_out)
    p.op("act", lambda e: e.activation(out, in_, func, **kw), outs, ins)


def act_copy(p, out, in_, okey=None, ikey=None, scale=None, bias=None):
    act(p, out, in_, AF.Copy if bias is None else AF.Identity,
        bias=bias, scale=scale, okey=okey, ikey=ikey)


def dve_copy(p, out, in_, okey=None, ikey=None):
    p.op("dve", lambda e: e.tensor_copy(out, in_), [okey or out], [ikey or in_])


def scan(p, out, d0, d1, initial, okey=None, keys=None):
    ins = [d0, d1] if keys is None else keys
    if isinstance(initial, bass.AP):
        ins = list(ins) + [initial]
    p.op("dve", lambda e: e.tensor_tensor_scan(out, d0, d1, initial,
                                               AluOpType.mult, AluOpType.add),
         [okey or out], ins)


def mm(p, out, lhsT, rhs, start=True, stop=True, okey=None, lkey=None, rkey=None):
    p.op("pe", lambda e: e.matmul(out, lhsT, rhs, start=start, stop=stop),
         [okey or out], [lkey or lhsT, rkey or rhs])


def mm_group(p, out, pairs, okey=None, ikeys=None, start=True, stop=True):
    fns = []
    n = len(pairs)
    for i, (l, r) in enumerate(pairs):
        fns.append(lambda e, l=l, r=r, s=(start and i == 0),
                   t=(stop and i == n - 1):
                   e.matmul(out, l, r, start=s, stop=t))
    ins = ikeys if ikeys is not None else [x for pr in pairs for x in pr]
    p.group("pe", fns, [okey or out], ins)


def transpose(p, out, in_, ident, okey=None, ikey=None):
    p.op("pe", lambda e: e.transpose(out, in_, ident),
         [okey or out], [ikey or in_, ident])


def memset(p, eng, out, val, okey=None):
    p.op(eng, lambda e: e.memset(out, val), [okey or out], [])


"""Megakernel builder for mk_hybrid_d1: full forward of
  x += attn_a(LN(x)); x += ffn(LN(x)); x += ssm(LN(x)); x += ffn(LN(x))
on one NeuronCore per batch element (8 cores, data-parallel over batch).

Layouts:
  residual x: DRAM (T=2048 rows t, D=2048 cols d) fp32, streamed in 128-row tiles
  ln_dm: SBUF (128 p=d%128, 16 dblk, T) bf16  (feature-major for matmuls)
  weights: host pre-tiled to (loop_idx, partition, free...) fp32; DMA-cast bf16
  scratch (per core, Internal DRAM): q_s/k_s (dmblk,p,t) bf16, v_s (tblk,p,dv),
  hid_s (hblk,p,t), w2p_s (p,B,o) [FWHT-folded W2'], hs_s/dt_s/sg_s/y_s
  (db,p,t) bf16, xA/xB fp32, bcflat_s.
"""


T = 2048
D = 2048
NH, HD = 16, 128
FH = 8192
DI = 4096
NS = 16
RK = 128
SQRT2 = 1.4142135623730951

ALL_PHASES = ("w2p", "attn_a", "ffn1", "attn_b", "ffn2")


def _slopes(n):
    start = 2.0 ** (-2.0 ** (-(math.log2(n) - 3)))
    return np.array([start ** (i + 1) for i in range(n)])


def _had(n):
    i = np.arange(n)
    H = np.empty((n, n), np.float32)
    for a in range(n):
        H[a] = np.array([(-1.0) ** bin(a & j).count("1") for j in i], np.float32)
    return H


def stage_shared(w):
    f = np.float32
    s = {}
    A = np.ascontiguousarray
    wqkv = w["L0_attn_a_w_qkv"]
    wqkT = wqkv.T
    qk = wqkT[:, :4096].reshape(16, 128, 32, 128)
    s["wqkT"] = A(qk.transpose(2, 1, 0, 3)).astype(f)
    vv = wqkT[:, 4096:].reshape(16, 128, 4, 512)
    s["wvT"] = A(vv.transpose(2, 1, 0, 3)).astype(f)
    bqkv = w["L0_attn_a_b_qkv"]
    s["bqk_cols"] = A(bqkv[:4096].reshape(32, 128).T).astype(f)
    s["bv_rep"] = A(np.broadcast_to(bqkv[4096:], (128, 2048))).astype(f)
    s["woutT_t"] = A(w["L0_attn_a_w_out"].T.reshape(16, 128, 2048)
                     .transpose(1, 0, 2)).astype(f)
    s["bout_rep"] = A(np.broadcast_to(w["L0_attn_a_b_out"], (128, 2048))).astype(f)
    w1T = w["L0_ffn_W1"].T
    s["w1T_t"] = A(w1T.reshape(16, 128, 64, 128).transpose(2, 1, 0, 3)).astype(f)
    w3T = w["L0_ffn_W3"].T
    s["w3T_t"] = A(w3T.reshape(16, 128, 64, 128).transpose(2, 1, 0, 3)).astype(f)
    w2T = w["L0_ffn_W2"].T
    s["w2T_t"] = A(w2T.reshape(64, 128, 16, 128).transpose(2, 1, 0, 3)).astype(f)
    s["winsT_t"] = A(w["L0_attn_b_W_in_states"].T.reshape(16, 128, 32, 128)
                     .transpose(2, 1, 0, 3)).astype(f)
    s["wingT_t"] = A(w["L0_attn_b_W_in_gates"].T.reshape(16, 128, 32, 128)
                     .transpose(2, 1, 0, 3)).astype(f)
    s["convw_cols"] = A(w["L0_attn_b_conv_w"].reshape(32, 128, 4)
                        .transpose(1, 0, 2).reshape(128, 128)).astype(f)
    s["convb_cols"] = A(w["L0_attn_b_conv_b"].reshape(32, 128).T).astype(f)
    s["wdtT_t"] = A(w["L0_attn_b_W_dt"].T.reshape(32, 128, 128)
                    .transpose(1, 0, 2)).astype(f)
    s["wdtpT_t"] = A(w["L0_attn_b_W_dtproj"].T.reshape(128, 32, 128)).astype(f)
    s["bdtp_cols"] = A(w["L0_attn_b_b_dtproj"].reshape(32, 128).T).astype(f)
    wbc = np.concatenate([w["L0_attn_b_W_b"], w["L0_attn_b_W_c"]], 0)
    s["wbcT_t"] = A(wbc.T.reshape(32, 128, 32).transpose(1, 0, 2)).astype(f)
    s["alog_cols"] = A(w["L0_attn_b_A_log"].reshape(32, 128, 16)
                       .transpose(1, 0, 2).reshape(128, 512)).astype(f)
    s["d_cols"] = A(w["L0_attn_b_D"].reshape(32, 128).T).astype(f)
    s["woutbT_t"] = A(w["L0_attn_b_W_out"].T.reshape(32, 128, 2048)
                      .transpose(1, 0, 2)).astype(f)
    lam = np.exp(-_slopes(NH))
    pos = np.arange(128, dtype=np.float64)
    dq = lam[:, None] ** (pos + 1.0)
    dk = lam[:, None] ** (127.0 - pos)
    diff = pos[:, None] - pos[None, :]
    dm = np.where(diff >= 0, lam[:, None, None] ** np.maximum(diff, 0), 0.0)
    s["dq_rep"] = A(np.broadcast_to(dq[None], (128, NH, 128))).astype(f)
    s["maskT"] = A(dm.transpose(2, 0, 1)).astype(f)
    s["dk_cols"] = A(dk.T).astype(f)
    s["h128"] = _had(128) * (2.0 ** -7)
    s["ident"] = np.eye(128, dtype=f)
    s["ones1"] = np.ones((1, 128), f)
    return s


def lam_c_floats():
    lam = np.exp(-_slopes(NH))
    return [float(x) for x in lam ** 128]


class C:
    def __init__(self, nc, p, stack, io):
        self.nc, self.p, self.stack, self.io = nc, p, stack, io
        self.sb_cache = {}

    def sb(self, name, shape, dt=F32):
        t = self.stack.enter_context(self.nc.sbuf_tensor(name, list(shape), dt))
        return t

    def ps(self, name, shape, dt=F32):
        return self.stack.enter_context(self.nc.psum_tensor(name, list(shape), dt))

    def dram(self, name, shape, dt, kind="Internal"):
        return self.nc.dram_tensor(name, list(shape), dt, kind=kind)


def build_init(c):
    p, io = c.p, c.io
    for name, shape, dt, src in [
        ("ident_f", (128, 128), F32, "ident"),
        ("ident_b", (128, 128), BF16, "ident"),
        ("h128_b", (128, 128), BF16, "h128"),
        ("bqk_cols", (128, 32), F32, "bqk_cols"),
        ("bv_rep", (128, 2048), BF16, "bv_rep"),
        ("bout_rep", (128, 2048), BF16, "bout_rep"),
        ("dq_rep", (128, NH, 128), BF16, "dq_rep"),
        ("maskT", (128, NH, 128), BF16, "maskT"),
        ("dk_cols", (128, NH), F32, "dk_cols"),
        ("convw_cols", (128, 128), F32, "convw_cols"),
        ("convb_cols", (128, 32), F32, "convb_cols"),
        ("bdtp_cols", (128, 32), F32, "bdtp_cols"),
        ("alog_cols", (128, 512), F32, "alog_cols"),
        ("d_cols", (128, 32), F32, "d_cols"),
    ]:
        t = c.sb("s_" + name, shape, dt)
        setattr(c, name, t)
        p.dma("qsp" if dt == F32 else "qpool", t[:], io[src])
    c.ones1 = c.sb("s_ones1", (1, 128), F32)
    p.dma("qsp", c.ones1[:], io["ones1"])
    c.c_eps = c.sb("c_eps", (128, 1), F32)
    memset(p, "pool", c.c_eps[:], 1e-5)
    c.c_one = c.sb("c_one", (128, 1), F32)
    memset(p, "pool", c.c_one[:], 1.0)
    c.aneg = c.sb("aneg", (128, 512), F32)
    act(p, c.aneg[:], c.alog_cols[:], AF.Exp)
    ts(p, c.aneg[:], c.aneg[:], -1.0, None, OP.mult)
    c.psA = c.ps("psA", (128, 2048), F32)
    c.psB = c.ps("psB", (128, 1024), F32)
    c.ptr = [c.ps("ptr0", (128, 128), BF16), c.ps("ptr1", (128, 128), BF16)]
    c.q_s = c.dram("q_s", (16, 128, T), BF16)
    c.k_s = c.dram("k_s", (16, 128, T), BF16)
    c.v_s = c.dram("v_s", (16, 128, D), BF16)
    c.hid_s = c.dram("hid_s", (64, 128, T), BF16)
    c.w2p_s = c.dram("w2p_s", (128, 64, 2048), BF16)
    c.hs_s = c.dram("hs_s", (32, 128, T), BF16)
    c.dt_s = c.dram("dt_s", (32, 128, T), BF16)
    c.sg_s = c.dram("sg_s", (32, 128, T), BF16)
    c.y_s = c.dram("y_s", (32, 128, T), BF16)
    c.bcflat_s = c.dram("bcflat_s", (2, 16, T), F32)
    c.xA = c.dram("xA", (T, D), F32)
    c.xB = c.dram("xB", (T, D), F32)


def psA(c, i, w=512):
    return c.psA[:, i * 512:i * 512 + w], f"psA{i}"


def psB(c, i, w=512):
    return c.psB[:, i * 512:i * 512 + w], f"psB{i}"


def gen_w2p(c, st):
    p, io = c.p, c.io
    OW = 128
    if True:
        wch = [st.enter_context(c.nc.sbuf_tensor("w2c0", [128, 64, OW], BF16))]
        fa = st.enter_context(c.nc.sbuf_tensor("w2fa", [128, 64, OW], F32))
        fb = st.enter_context(c.nc.sbuf_tensor("w2fb", [128, 64, OW], F32))
        outb = st.enter_context(c.nc.sbuf_tensor("w2ob", [128, 64, OW], BF16))
        for och in range(16):
            wc = wch[0]
            p.dma("qpool", wc[:], io["w2T_t"][och], slot="w2c0")
            beng = "dve"
            for B in range(64):
                pa = c.psA[:, (2 + B % 2) * 512:(2 + B % 2) * 512 + OW]
                key = f"hb{2 + B % 2}"
                mm(p, pa, c.h128_b[:], wc[:, B, :], okey=key,
                   lkey="h128_b", rkey="w2c0")
                if och % 2 == 0:
                    p.op("dve", lambda e, o=fa[:, B, :], i_=pa:
                         e.tensor_scalar(o, i_, SQRT2, None, OP.mult),
                         ["w2fa"], [key])
                else:
                    act(p, fa[:, B, :], pa, AF.Copy, scale=SQRT2,
                        okey="w2fa", ikey=key)
            bufs = [fa, fb]
            cur = 0
            for si, h in enumerate([1, 2, 4, 8, 16, 32]):
                src, dst = bufs[cur], bufs[1 - cur]
                last = si == 5
                if last:
                    dst = outb
                sap = src[:].rearrange("p (g x j) o -> p g x j o", x=2, j=h)
                dap = dst[:].rearrange("p (g x j) o -> p g x j o", x=2, j=h)
                p.op(beng, lambda e, o=dap[:, :, 0], a=sap[:, :, 0],
                     b=sap[:, :, 1]: e.tensor_tensor(o, a, b, OP.add),
                     [dst.name], [src.name])
                p.op(beng, lambda e, o=dap[:, :, 1], a=sap[:, :, 0],
                     b=sap[:, :, 1]: e.tensor_tensor(o, a, b, OP.subtract),
                     [dst.name], [src.name])
                cur = 1 - cur
            for bb in range(0, 64, 8):
                p.dma("qact", c.w2p_s.ap()[:, bb:bb + 8, och * OW:(och + 1) * OW],
                      outb[:, bb:bb + 8, :], slot="w2p_s", dst_key="w2p_s",
                      src_key="w2ob")
            yield


def build_w2p(c):
    with ExitStack() as st:
        for _ in gen_w2p(c, st):
            pass
        c.p.fence()


def emit_ln_dm(c, x_src, ln_dm, tag):
    p = c.p
    with ExitStack() as st:
        xt = [st.enter_context(c.nc.sbuf_tensor(f"lnx{tag}{i}", [128, D], F32))
              for i in range(2)]
        lnt = [st.enter_context(c.nc.sbuf_tensor(f"lnt{tag}{i}", [128, D], BF16))
               for i in range(2)]
        st6 = [st.enter_context(c.nc.sbuf_tensor(f"st6{tag}{i}", [128, 4, 6], F32))
               for i in range(2)]
        st2 = [st.enter_context(c.nc.sbuf_tensor(f"st2{tag}{i}", [128, 2], F32))
               for i in range(2)]
        ngm = [st.enter_context(c.nc.sbuf_tensor(f"ngm{tag}{i}", [128, 1], F32))
               for i in range(2)]
        rsd = [st.enter_context(c.nc.sbuf_tensor(f"rsd{tag}{i}", [128, 1], F32))
               for i in range(2)]
        p.dma("qsp", xt[0][:], x_src[0:128, :], slot=f"lnx{tag}0")
        for tb in range(16):
            i = tb % 2
            if tb < 15:
                ii = (tb + 1) % 2
                p.dma("qsp", xt[ii][:],
                      x_src[(tb + 1) * 128:(tb + 2) * 128, :],
                      slot=f"lnx{tag}{ii}")
            for k in range(4):
                p.op("dve", lambda e, a=st6[i], b=xt[i], k=k:
                     e.bn_stats(a[:, k, :], b[:, k * 512:(k + 1) * 512]),
                     [st6[i]], [xt[i]])
            p.op("dve", lambda e, a=st2[i], b=st6[i]:
                 e.bn_aggr(a[:], b[:].rearrange("p a b -> p (a b)")),
                 [st2[i]], [st6[i]])
            ts(p, ngm[i][:], st2[i][:, 0:1], -1.0, None, OP.mult,
               okey=ngm[i].name, ikey=st2[i].name)
            act(p, rsd[i][:], st2[i][:, 1:2], AF.Sqrt, bias=c.c_eps[:],
                okey=rsd[i].name, ikey=st2[i].name)
            p.op("dve", lambda e, a=rsd[i]: e.reciprocal(a[:], a[:]),
                 [rsd[i]], [rsd[i]])
            ts(p, lnt[i][:], xt[i][:], ngm[i][:], rsd[i][:], OP.add, OP.mult)
            for db in range(16):
                tr = c.ptr[db % 2]
                transpose(p, tr[:], lnt[i][:, db * 128:(db + 1) * 128],
                          c.ident_b[:], okey=tr.name, ikey=lnt[i].name)
                act_copy(p, ln_dm[:, db, tb * 128:(tb + 1) * 128], tr[:],
                         okey=f"{ln_dm.name}.{tb // 4}", ikey=tr.name)
        p.fence()


def build_attn_a(c, x_src, x_dst, x_dst_name, with_w2p=False):
    p, io = c.p, c.io
    lam_c = lam_c_floats()
    with ExitStack() as st:
        ln_dm = st.enter_context(c.nc.sbuf_tensor("ln_a", [128, 16, T], BF16))
        emit_ln_dm(c, x_src, ln_dm, "a")
        with ExitStack() as st2:
            w2st = ExitStack()
            w2gen = gen_w2p(c, w2st) if with_w2p else None
            wqk = [st2.enter_context(c.nc.sbuf_tensor(f"wqk{i}", [128, 16, 128],
                                                      BF16)) for i in range(2)]
            qkt = [st2.enter_context(c.nc.sbuf_tensor(f"qkt{i}", [128, 512],
                                                      BF16)) for i in range(2)]
            p.dma("qpool", wqk[0][:], io["wqkT"][0], slot=wqk[0].name)
            for ms in range(32):
                if w2gen is not None:
                    next(w2gen, None)
                if ms < 31:
                    wn = wqk[(ms + 1) % 2]
                    p.dma("qpool", wn[:], io["wqkT"][ms + 1], slot=wn.name)
                wb = wqk[ms % 2]
                dst = c.q_s if ms < 16 else c.k_s
                blk = ms % 16
                for tn in range(4):
                    j4 = (ms * 4 + tn) % 4
                    j = (ms * 4 + tn) % 2
                    pb, key = psB(c, j4) if j4 < 2 else psA(c, j4 - 2)
                    mm_group(p, pb, [(wb[:, kb, :],
                                      ln_dm[:, kb, tn * 512:(tn + 1) * 512])
                                     for kb in range(16)],
                             okey=key, ikeys=[wb.name, f"ln_a.{tn}"])
                    act(p, qkt[j][:], pb, AF.Identity,
                        bias=c.bqk_cols[:, ms:ms + 1], okey=qkt[j].name, ikey=key)
                    p.dma("qact", dst.ap()[blk, :, tn * 512:(tn + 1) * 512],
                          qkt[j][:], slot=dst.name, dst_key=dst.name,
                          src_key=qkt[j].name)
            if w2gen is not None:
                for _ in w2gen:
                    pass
                w2st.close()
                p.fence()
            wv = [st2.enter_context(c.nc.sbuf_tensor(f"wv{i}", [128, 16, 512],
                                                     BF16)) for i in range(2)]
            vt = [st2.enter_context(c.nc.sbuf_tensor(f"vt{i}", [128, 512], BF16))
                  for i in range(2)]
            p.dma("qpool", wv[0][:], io["wvT"][0], slot=wv[0].name)
            for dv in range(4):
                if dv < 3:
                    wn = wv[(dv + 1) % 2]
                    p.dma("qpool", wn[:], io["wvT"][dv + 1], slot=wn.name)
                wb = wv[dv % 2]
                for tsl in range(16):
                    j4 = (dv * 16 + tsl) % 4
                    j = (dv * 16 + tsl) % 2
                    pb, key = psB(c, j4) if j4 < 2 else psA(c, j4 - 2)
                    mm_group(p, pb, [(ln_dm[:, kb, tsl * 128:(tsl + 1) * 128],
                                      wb[:, kb, :]) for kb in range(16)],
                             okey=key, ikeys=[wb.name, f"ln_a.{tsl // 4}"])
                    tt(p, vt[j][:], pb, c.bv_rep[:, dv * 512:(dv + 1) * 512],
                       OP.add, okey=vt[j].name, akey=key, bkey="bv_rep")
                    p.dma("qact", c.v_s.ap()[tsl, :, dv * 512:(dv + 1) * 512],
                          vt[j][:], slot="v_s", dst_key="v_s",
                          src_key=vt[j].name)
            p.fence()
    with ExitStack() as st:
        nsb = c.nc.sbuf_tensor
        wout = st.enter_context(nsb("wout_r", [128, 16, 2048], BF16))
        for i in range(4):
            p.dma("qpool", wout[:, i * 4:(i + 1) * 4, :],
                  io["woutT_t"][:, i * 4:(i + 1) * 4, :], slot="wout_r",
                  dst_key="wout_r")
        qc = st.enter_context(nsb("qc", [128, 16, 512], BF16))
        kc = st.enter_context(nsb("kc", [128, 16, 512], BF16))
        vc = st.enter_context(nsb("vc", [128, 4, 2048], BF16))
        S_f = st.enter_context(nsb("S_f", [128, NH, 128], F32))
        S_b = st.enter_context(nsb("S_b", [128, NH, 128], BF16))
        Stmp = [st.enter_context(nsb(f"Stmp{i}", [128, 128], F32)) for i in range(4)]
        smask = [st.enter_context(nsb(f"smask{i}", [128, 128], BF16)) for i in range(4)]
        ktdk = [st.enter_context(nsb(f"ktdk{i}", [128, 128], BF16)) for i in range(4)]
        qdq = [st.enter_context(nsb(f"qdq{i}", [128, 128], BF16)) for i in range(4)]
        attn = st.enter_context(nsb("attn_dm", [128, 16, 512], BF16))
        xt = st.enter_context(nsb("xta", [128, 2048], F32))
        xnew = st.enter_context(nsb("xna", [128, 2048], F32))
        p.op("dve", lambda e: e.memset(S_f[:], 0.0),
             [f"S_f.{h}" for h in range(NH)], [])
        p.op("dve", lambda e: e.memset(S_b[:], 0.0),
             [f"S_b.{h}" for h in range(NH)], [])
        for tch in range(4):
            sl = slice(tch * 512, (tch + 1) * 512)
            for g in range(4):
                p.dma("qsp", qc[:, g * 4:(g + 1) * 4, :],
                      c.q_s.ap()[g * 4:(g + 1) * 4, :, sl]
                      .rearrange("b p t -> p b t"), slot="qc", dst_key="qc",
                      src_key="q_s")
                p.dma("qsp", kc[:, g * 4:(g + 1) * 4, :],
                      c.k_s.ap()[g * 4:(g + 1) * 4, :, sl]
                      .rearrange("b p t -> p b t"), slot="kc", dst_key="kc",
                      src_key="k_s")
            for i in range(4):
                p.dma("qsp", vc[:, i, :], c.v_s.ap()[tch * 4 + i],
                      slot="vc", dst_key="vc", src_key="v_s")
            for lc in range(4):
                cs = slice(lc * 128, (lc + 1) * 128)
                def hbank(b):
                    if b < 4:
                        return c.psA[:, b * 512:b * 512 + 128], f"hb{b}"
                    return c.psB[:, (b - 4) * 512:(b - 4) * 512 + 128], f"hb{b}"

                def stage1(h):
                    b = h % 6
                    pbk, key = hbank(b)
                    mm(p, pbk, kc[:, h, cs], qc[:, h, cs], okey=key,
                       lkey="kc", rkey="qc")
                    tr = c.ptr[h % 2]
                    transpose(p, tr[:], kc[:, h, cs], c.ident_b[:],
                              okey=f"ptr{h % 2}", ikey="kc")

                def stage2(h):
                    b = h % 6
                    pbk, key = hbank(b)
                    tt(p, smask[h % 4][:], pbk, c.maskT[:, h, :], OP.mult,
                       okey=smask[h % 4].name, akey=key, bkey="maskT")
                    ts(p, ktdk[h % 4][:], c.ptr[h % 2][:],
                       c.dk_cols[:, h:h + 1], None, OP.mult,
                       okey=ktdk[h % 4].name, ikey=f"ptr{h % 2}")
                    tt(p, qdq[h % 4][:], qc[:, h, cs], c.dq_rep[:, h, :],
                       OP.mult, okey=qdq[h % 4].name, akey="qc", bkey="dq_rep")

                def stage3(h):
                    b = h % 6
                    pbk, key = hbank(b)
                    v_t = vc[:, lc, h * 128:(h + 1) * 128]
                    mm(p, pbk, S_b[:, h, :], qdq[h % 4][:],
                       start=True, stop=False, okey=key, lkey=f"S_b.{h}",
                       rkey=qdq[h % 4].name)
                    mm(p, pbk, v_t, smask[h % 4][:], start=False, stop=True,
                       okey=key, lkey="vc", rkey=smask[h % 4].name)

                def stage4(h):
                    b = h % 6
                    pbk, key = hbank(b)
                    act_copy(p, attn[:, h, cs], pbk, okey=f"attn.{lc}",
                             ikey=key)

                def stage5(h):
                    b = h % 6
                    pbk, key = hbank(b)
                    v_t = vc[:, lc, h * 128:(h + 1) * 128]
                    mm(p, pbk, ktdk[h % 4][:], v_t, okey=key,
                       lkey=ktdk[h % 4].name, rkey="vc")
                    ts(p, Stmp[h % 4][:], S_f[:, h, :], lam_c[h], None,
                       OP.mult, okey=Stmp[h % 4].name, ikey=f"S_f.{h}")
                    tt(p, S_f[:, h, :], Stmp[h % 4][:], pbk, OP.add,
                       okey=f"S_f.{h}", akey=Stmp[h % 4].name, bkey=key)
                    act_copy(p, S_b[:, h, :], S_f[:, h, :], okey=f"S_b.{h}",
                             ikey=f"S_f.{h}")

                stage1(0)
                stage2(0)
                for h in range(NH):
                    if h + 1 < NH:
                        stage1(h + 1)
                        stage2(h + 1)
                    stage3(h)
                    stage4(h)
                    stage5(h)
            for tsl in range(4):
                row = tch * 512 + tsl * 128
                p.dma("qsp", xt[:], x_src[row:row + 128, :], slot="xta")
                for oc in range(4):
                    if oc % 2 == 0:
                        pb = c.psB[:, 512:1024]
                        keys = ["hb5"]
                    else:
                        pb = c.psA[:, 1536:2048]
                        keys = ["hb3"]
                    fns = []
                    for db in range(16):
                        fns.append(lambda e, o=pb, l=attn[:, db, tsl * 128:
                                                         (tsl + 1) * 128],
                                   r=wout[:, db, oc * 512:(oc + 1) * 512],
                                   s=(db == 0), t=(db == 15):
                                   e.matmul(o, l, r, start=s, stop=t))
                    p.group("pe", fns, keys, [f"attn.{tsl}", "wout_r"])
                    osl = slice(oc * 512, (oc + 1) * 512)
                    tt(p, xnew[:, osl], pb, c.bout_rep[:, osl], OP.add,
                       okey="xna", akey=keys[0], bkey="bout_rep")
                    tt(p, xnew[:, osl], xnew[:, osl], xt[:, osl], OP.add,
                       okey="xna", akey="xna", bkey="xta")
                p.dma("qact", x_dst[row:row + 128, :], xnew[:],
                      slot=x_dst_name, dst_key=x_dst_name, src_key="xna")
        p.fence()


def build_ffn(c, x_src, x_dst, x_dst_name, tag):
    p, io = c.p, c.io
    with ExitStack() as st:
        nsb = c.nc.sbuf_tensor
        ln_dm = st.enter_context(nsb(f"ln_f{tag}", [128, 16, T], BF16))
        emit_ln_dm(c, x_src, ln_dm, f"f{tag}")
        with ExitStack() as st2:
            w1b = [st2.enter_context(nsb(f"w1b{tag}{i}", [128, 16, 128], BF16))
                   for i in range(2)]
            w3b = [st2.enter_context(nsb(f"w3b{tag}{i}", [128, 16, 128], BF16))
                   for i in range(2)]
            gt = [st2.enter_context(nsb(f"gt{tag}{i}", [128, 512], F32))
                  for i in range(2)]
            ht = [st2.enter_context(nsb(f"ht{tag}{i}", [128, 512], BF16))
                  for i in range(2)]
            p.dma("qpool", w1b[0][:], io["w1T_t"][0], slot=w1b[0].name)
            p.dma("qpool", w3b[0][:], io["w3T_t"][0], slot=w3b[0].name)
            for hs_ in range(64):
                i = hs_ % 2
                if hs_ < 63:
                    ii = (hs_ + 1) % 2
                    p.dma("qpool", w1b[ii][:], io["w1T_t"][hs_ + 1],
                          slot=w1b[ii].name)
                    p.dma("qpool", w3b[ii][:], io["w3T_t"][hs_ + 1],
                          slot=w3b[ii].name)
                for tn in range(4):
                    tsl = slice(tn * 512, (tn + 1) * 512)
                    j = tn % 2
                    g_ps, g_key = psA(c, j)
                    l_ps, l_key = psA(c, 2 + j)
                    mm_group(p, g_ps, [(w1b[i][:, kb, :], ln_dm[:, kb, tsl])
                                       for kb in range(16)],
                             okey=g_key, ikeys=[w1b[i].name, f"{ln_dm.name}.{tn}"])
                    mm_group(p, l_ps, [(w3b[i][:, kb, :], ln_dm[:, kb, tsl])
                                       for kb in range(16)],
                             okey=l_key, ikeys=[w3b[i].name, f"{ln_dm.name}.{tn}"])
                    act(p, gt[j][:], g_ps, AF.Silu, okey=gt[j].name, ikey=g_key)
                    tt(p, ht[j][:], gt[j][:], l_ps, OP.mult, okey=ht[j].name,
                       akey=gt[j].name, bkey=l_key)
                    p.dma("qact", c.hid_s.ap()[hs_, :, tsl], ht[j][:],
                          slot="hid_s", dst_key="hid_s", src_key=ht[j].name)
            p.fence()
    with ExitStack() as st:
        nsb = c.nc.sbuf_tensor
        w2po = st.enter_context(nsb(f"w2po{tag}", [128, 64, 512], BF16))
        hidb = [st.enter_context(nsb(f"hidb{tag}{i}", [128, 64, 256], BF16))
                for i in range(2)]
        xoc = [st.enter_context(nsb(f"xoc{tag}{i}", [128, 512], F32)) for i in range(2)]
        xpt = [st.enter_context(nsb(f"xpt{tag}{i}", [128, 512], F32)) for i in range(2)]
        for oc in range(4):
            for bb in range(0, 64, 8):
                p.dma("qsp", w2po[:, bb:bb + 8, :],
                      c.w2p_s.ap()[:, bb:bb + 8, oc * 512:(oc + 1) * 512],
                      slot=w2po.name, dst_key=w2po.name, src_key="w2p_s")
            for tch in range(8):
                i = tch % 2
                for bb in range(0, 64, 16):
                    p.dma("qsp", hidb[i][:, bb:bb + 16, :],
                          c.hid_s.ap()[bb:bb + 16, :,
                                       tch * 256:(tch + 1) * 256]
                          .rearrange("b p t -> p b t"),
                          slot=hidb[i].name, dst_key=hidb[i].name,
                          src_key="hid_s")
                for tsl in range(2):
                    j = (tch * 2 + tsl) % 2
                    row = tch * 256 + tsl * 128
                    p.dma("qsp", xoc[j][:], x_src[row:row + 128,
                                                  oc * 512:(oc + 1) * 512],
                          slot=xoc[j].name)
                    pb, key = psB(c, j)
                    mm_group(p, pb, [(hidb[i][:, hb, tsl * 128:(tsl + 1) * 128],
                                      w2po[:, hb, :]) for hb in range(64)],
                             okey=key, ikeys=[hidb[i].name, w2po.name])
                    tt(p, xpt[j][:], pb, xoc[j][:], OP.add, okey=xpt[j].name,
                       akey=key, bkey=xoc[j].name)
                    p.dma("qact", x_dst[row:row + 128, oc * 512:(oc + 1) * 512],
                          xpt[j][:], slot=x_dst_name, dst_key=x_dst_name,
                          src_key=xpt[j].name)
        p.fence()


def build_attn_b(c, x_src, x_dst, x_dst_name):
    p, io = c.p, c.io
    with ExitStack() as st:
        nsb = c.nc.sbuf_tensor
        ln_dm = st.enter_context(nsb("ln_b", [128, 16, T], BF16))
        emit_ln_dm(c, x_src, ln_dm, "b")
        with ExitStack() as st2:
            wdt = st2.enter_context(nsb("wdt_r", [128, 32, 128], BF16))
            p.dma("qpool", wdt[:], io["wdtT_t"], slot="wdt_r")
            wdtp = st2.enter_context(nsb("wdtp_r", [128, 32, 128], BF16))
            p.dma("qpool", wdtp[:], io["wdtpT_t"], slot="wdtp_r")
            wins = [st2.enter_context(nsb(f"wins{i}", [128, 16, 128], BF16))
                    for i in range(2)]
            wing = [st2.enter_context(nsb(f"wing{i}", [128, 16, 128], BF16))
                    for i in range(2)]
            hs0 = [st2.enter_context(nsb(f"hs0_{i}", [128, 2051], F32))
                   for i in range(2)]
            cv = [st2.enter_context(nsb(f"cv{i}", [128, 2048], F32))
                  for i in range(2)]
            hsb = [st2.enter_context(nsb(f"hsb{i}", [128, 2048], BF16))
                   for i in range(2)]
            sgb = [st2.enter_context(nsb(f"sgb{i}", [128, 2048], BF16))
                   for i in range(2)]
            memset(p, "dve", hs0[0][:, 0:3], 0.0)
            memset(p, "dve", hs0[1][:, 0:3], 0.0)
            p.dma("qpool", wins[0][:], io["winsT_t"][0], slot=wins[0].name)
            p.dma("qpool", wing[0][:], io["wingT_t"][0], slot=wing[0].name)
            for db in range(32):
                i = db % 2
                if db < 31:
                    ii = (db + 1) % 2
                    p.dma("qpool", wins[ii][:], io["winsT_t"][db + 1],
                          slot=wins[ii].name)
                    p.dma("qpool", wing[ii][:], io["wingT_t"][db + 1],
                          slot=wing[ii].name)
                for tn in range(4):
                    j = tn % 2
                    tsl = slice(tn * 512, (tn + 1) * 512)
                    pb, key = psB(c, j)
                    mm_group(p, pb, [(wins[i][:, kb, :], ln_dm[:, kb, tsl])
                                     for kb in range(16)],
                             okey=key, ikeys=[wins[i].name, f"ln_b.{tn}"])
                    act_copy(p, hs0[i][:, 3 + tn * 512: 3 + (tn + 1) * 512],
                             pb, okey=hs0[i].name, ikey=key)
                ts(p, cv[0][:], hs0[i][:, 0:2048],
                   c.convw_cols[:, db * 4:db * 4 + 1],
                   c.convb_cols[:, db:db + 1], OP.mult, OP.add,
                   okey="cv0", ikey=hs0[i].name)
                stt(p, cv[1][:], hs0[i][:, 1:2049],
                    c.convw_cols[:, db * 4 + 1:db * 4 + 2], cv[0][:],
                    OP.mult, OP.add, okey="cv1")
                stt(p, cv[0][:], hs0[i][:, 2:2050],
                    c.convw_cols[:, db * 4 + 2:db * 4 + 3], cv[1][:],
                    OP.mult, OP.add, okey="cv0")
                stt(p, cv[1][:], hs0[i][:, 3:2051],
                    c.convw_cols[:, db * 4 + 3:db * 4 + 4], cv[0][:],
                    OP.mult, OP.add, okey="cv1")
                act(p, hsb[i][:], cv[1][:], AF.Silu, okey=hsb[i].name, ikey="cv1")
                p.dma("qact", c.hs_s.ap()[db], hsb[i][:], slot="hs_s",
                      dst_key="hs_s", src_key=hsb[i].name)
                for tn in range(4):
                    j = tn % 2
                    tsl = slice(tn * 512, (tn + 1) * 512)
                    pb, key = psB(c, j)
                    mm_group(p, pb, [(wing[i][:, kb, :], ln_dm[:, kb, tsl])
                                     for kb in range(16)],
                             okey=key, ikeys=[wing[i].name, f"ln_b.{tn}"])
                    act(p, sgb[i][:, tsl], pb, AF.Silu, okey=sgb[i].name,
                        ikey=key)
                p.dma("qact", c.sg_s.ap()[db], sgb[i][:], slot="sg_s",
                      dst_key="sg_s", src_key=sgb[i].name)
                for tn in range(4):
                    pa, key = psA(c, tn)
                    mm(p, pa, wdt[:, db, :],
                       hsb[i][:, tn * 512:(tn + 1) * 512],
                       start=(db == 0), stop=(db == 31),
                       okey=key, lkey="wdt_r", rkey=hsb[i].name)
            tsbf = st2.enter_context(nsb("tsbf", [128, 2048], BF16))
            for tn in range(4):
                pa, key = psA(c, tn)
                act_copy(p, tsbf[:, tn * 512:(tn + 1) * 512], pa,
                         okey="tsbf", ikey=key)
            et = [st2.enter_context(nsb(f"et{i}", [128, 512], F32))
                  for i in range(2)]
            dtb = [st2.enter_context(nsb(f"dtb{i}", [128, 512], BF16))
                   for i in range(2)]
            for db in range(32):
                for tn in range(4):
                    j = (db * 4 + tn) % 2
                    pb, key = psB(c, j)
                    mm(p, pb, wdtp[:, db, :], tsbf[:, tn * 512:(tn + 1) * 512],
                       okey=key, lkey="wdtp_r", rkey="tsbf")
                    act(p, et[j][:], pb, AF.Exp, bias=c.bdtp_cols[:, db:db + 1],
                        okey=et[j].name, ikey=key)
                    act(p, dtb[j][:], et[j][:], AF.Ln, bias=c.c_one[:],
                        okey=dtb[j].name, ikey=et[j].name)
                    p.dma("qact", c.dt_s.ap()[db, :, tn * 512:(tn + 1) * 512],
                          dtb[j][:], slot="dt_s", dst_key="dt_s",
                          src_key=dtb[j].name)
            p.fence()
    nsb = c.nc.sbuf_tensor
    with ExitStack() as st2:
        wbc = st2.enter_context(nsb("wbc_r", [128, 32, 32], BF16))
        p.dma("qpool", wbc[:], io["wbcT_t"], slot="wbc_r")
        hsb2 = [st2.enter_context(nsb(f"hsc{i}", [128, 2048], BF16))
                for i in range(2)]
        p.dma("qsp", hsb2[0][:], c.hs_s.ap()[0], slot=hsb2[0].name)
        for db in range(32):
            i = db % 2
            if db < 31:
                ii = (db + 1) % 2
                p.dma("qsp", hsb2[ii][:], c.hs_s.ap()[db + 1],
                      slot=hsb2[ii].name)
            for tsl in range(16):
                mm(p, c.psB[:, tsl * 32:(tsl + 1) * 32],
                   hsb2[i][:, tsl * 128:(tsl + 1) * 128], wbc[:, db, :],
                   start=(db == 0), stop=(db == 31),
                   okey="psB0", lkey=hsb2[i].name, rkey="wbc_r")
        bc_sb = st2.enter_context(nsb("bc_sb", [128, 512], F32))
        act_copy(p, bc_sb[:], c.psB[:, 0:512], okey="bc_sb", ikey="psB0")
        bcT = st2.enter_context(nsb("bcT", [32, 16, 128], F32))
        for tsl in range(16):
            transpose(p, c.psB[0:32, 512:640], bc_sb[:, tsl * 32:(tsl + 1) * 32],
                      c.ident_f[:], okey="psB1", ikey="bc_sb")
            act_copy(p, bcT[:, tsl, :], c.psB[0:32, 512:640],
                     okey="bcT", ikey="psB1")
        p.dma("qact", c.bcflat_s.ap().rearrange("b n t -> (b n) t"),
              bcT[:].rearrange("p a b -> p (a b)"), slot="bcflat_s",
              dst_key="bcflat_s", src_key="bcT")
        p.fence()
    with ExitStack() as st2:
        B_rep = st2.enter_context(nsb("B_rep", [128, NS, 1024], BF16))
        C_rep = st2.enter_context(nsb("C_rep", [128, NS, 1024], BF16))
        dAv = st2.enter_context(nsb("dAv", [128, 2, 1024], F32))
        dBu = [st2.enter_context(nsb(f"dBu{i}", [128, NS, 1024], BF16))
               for i in range(2)]
        hv = st2.enter_context(nsb("hv", [128, 2, 1024], BF16))
        carry = st2.enter_context(nsb("carry", [128, 512], F32))
        dtb2 = [st2.enter_context(nsb(f"dtc{i}", [128, 1024], BF16))
                for i in range(2)]
        hsb3 = [st2.enter_context(nsb(f"hsd{i}", [128, 1024], BF16))
                for i in range(2)]
        sgb2 = [st2.enter_context(nsb(f"sgc{i}", [128, 1024], BF16))
                for i in range(2)]
        dtu = [st2.enter_context(nsb(f"dtu{i}", [128, 1024], BF16))
               for i in range(2)]
        t1 = st2.enter_context(nsb("t1_s", [128, 1024], F32))
        ybf = [st2.enter_context(nsb(f"ybf{i}", [128, 1024], BF16))
               for i in range(2)]
        memset(p, "dve", carry[:], 0.0)

        def stage_in(db, tch):
            i = db % 2
            tsl = slice(tch * 1024, (tch + 1) * 1024)
            p.dma("qsp", dtb2[i][:], c.dt_s.ap()[db, :, tsl],
                  slot=dtb2[i].name)
            p.dma("qsp", hsb3[i][:], c.hs_s.ap()[db, :, tsl],
                  slot=hsb3[i].name)
            p.dma("qsp", sgb2[i][:], c.sg_s.ap()[db, :, tsl],
                  slot=sgb2[i].name)
            tt(p, dtu[i][:], dtb2[i][:], hsb3[i][:], OP.mult,
               okey=dtu[i].name, akey=dtb2[i].name, bkey=hsb3[i].name)
            dtu_b = bass.AP(dtu[i], 0, [[1024, 128], [0, NS // 2], [1, 1024]])
            for half in range(2):
                hn = slice(half * 8, half * 8 + 8)
                p.op("pool", lambda e, o=dBu[i][:, hn, :],
                     a=B_rep[:, hn, :], bb=dtu_b:
                     e.tensor_tensor(o, a, bb, OP.mult),
                     [f"{dBu[i].name}.{half}"], [B_rep, dtu[i]])

        for tch in range(2):
            tsl = slice(tch * 1024, (tch + 1) * 1024)
            for bc, rep, rname in ((0, B_rep, "B_rep"), (1, C_rep, "C_rep")):
                srcap = bass.AP(c.bcflat_s, bc * NS * T + tch * 1024,
                                [[0, 128], [T, NS], [1, 1024]])
                p.dma("qpool", rep[:], srcap, slot=rname,
                      src_key="bcflat_s")
            stage_in(0, tch)
            for db in range(32):
                i = db % 2
                if db < 31:
                    stage_in(db + 1, tch)
                for n in range(NS):
                    s = n % 2
                    act(p, dAv[:, s, :], dtb2[i][:], AF.Exp,
                        scale=c.aneg[:, db * 16 + n:db * 16 + n + 1],
                        okey=f"dA{s}", ikey=dtb2[i].name)
                    scan(p, hv[:, s, :], dAv[:, s, :], dBu[i][:, n, :],
                         carry[:, db * 16 + n:db * 16 + n + 1],
                         okey=f"hv{s}",
                         keys=[f"dA{s}", f"{dBu[i].name}.{n // 8}", "carry"])
                    dve_copy(p, carry[:, db * 16 + n:db * 16 + n + 1],
                             hv[:, s, 1023:1024], okey="carry",
                             ikey=f"hv{s}")
                    tt(p, dBu[i][:, n, :], hv[:, s, :], C_rep[:, n, :],
                       OP.mult, okey=f"{dBu[i].name}.{n // 8}",
                       akey=f"hv{s}", bkey="C_rep")
                k0, k1 = f"{dBu[i].name}.0", f"{dBu[i].name}.1"
                p.op("dve", lambda e, o=dBu[i][:, 0:8, :], a=dBu[i][:, 0:8, :],
                     b=dBu[i][:, 8:16, :]: e.tensor_tensor(o, a, b, OP.add),
                     [k0], [k0, k1])
                p.op("dve", lambda e, o=dBu[i][:, 8:12, :], a=dBu[i][:, 0:4, :],
                     b=dBu[i][:, 4:8, :]: e.tensor_tensor(o, a, b, OP.add),
                     [k1], [k0])
                p.op("dve", lambda e, o=dBu[i][:, 12:14, :],
                     a=dBu[i][:, 8:10, :], b=dBu[i][:, 10:12, :]:
                     e.tensor_tensor(o, a, b, OP.add), [k1], [k1])
                ts(p, t1[:], hsb3[i][:], c.d_cols[:, db:db + 1], None,
                   OP.mult, okey="t1_s", ikey=hsb3[i].name)
                tt(p, t1[:], t1[:], dBu[i][:, 12, :], OP.add,
                   okey="t1_s", akey="t1_s", bkey=f"{dBu[i].name}.1")
                tt(p, t1[:], t1[:], dBu[i][:, 13, :], OP.add,
                   okey="t1_s", akey="t1_s", bkey=f"{dBu[i].name}.1")
                tt(p, ybf[i][:], t1[:], sgb2[i][:], OP.mult,
                   okey=ybf[i].name, akey="t1_s", bkey=sgb2[i].name)
                p.dma("qact", c.y_s.ap()[db, :, tsl], ybf[i][:],
                      slot="y_s", dst_key="y_s", src_key=ybf[i].name)
        p.fence()
    with ExitStack() as st:
        nsb = c.nc.sbuf_tensor
        woutb = st.enter_context(nsb("woutb_r", [128, 32, 2048], BF16))
        for i in range(8):
            p.dma("qpool", woutb[:, i * 4:(i + 1) * 4, :],
                  c.io["woutbT_t"][:, i * 4:(i + 1) * 4, :], slot="woutb_r",
                  dst_key="woutb_r")
        yb = st.enter_context(nsb("yb", [128, 32, 512], BF16))
        xt = st.enter_context(nsb("xtb", [128, 2048], F32))
        xnew = st.enter_context(nsb("xnb", [128, 2048], F32))
        for tch in range(4):
            tsl = slice(tch * 512, (tch + 1) * 512)
            for db in range(32):
                p.dma("qsp", yb[:, db, :], c.y_s.ap()[db, :, tsl],
                      slot="yb", dst_key="yb", src_key="y_s")
            for tq in range(4):
                row = tch * 512 + tq * 128
                p.dma("qsp", xt[:], x_src[row:row + 128, :], slot="xtb")
                for oc in range(4):
                    pb, key = psB(c, oc % 2)
                    mm_group(p, pb, [(yb[:, db, tq * 128:(tq + 1) * 128],
                                      woutb[:, db, oc * 512:(oc + 1) * 512])
                                     for db in range(32)],
                             okey=key, ikeys=["yb", "woutb_r"])
                    osl = slice(oc * 512, (oc + 1) * 512)
                    tt(p, xnew[:, osl], pb, xt[:, osl], OP.add,
                       okey="xnb", akey=key, bkey="xtb")
                p.dma("qact", x_dst[row:row + 128, :], xnew[:],
                      slot=x_dst_name, dst_key=x_dst_name, src_key="xnb")
        p.fence()


def build_program(phases=ALL_PHASES, x_out_taps=()):
    nc = bass.Bass("TRN2", target_bir_lowering=False)
    shared_names = list(stage_shared_dummy().keys())
    io = {}
    for name, arr in stage_shared_dummy().items():
        io[name] = nc.dram_tensor(name, list(arr.shape),
                                  mybir.dt.from_np(arr.dtype),
                                  kind="ExternalInput").ap()
    x0 = nc.dram_tensor("x0", [T, D], F32, kind="ExternalInput").ap()
    out = nc.dram_tensor("out", [T, D], F32, kind="ExternalOutput").ap()
    with ExitStack() as stack:
        p = Prog(nc)
        stack.callback(p.close)
        c = C(nc, p, stack, io)
        build_init(c)
        xA, xB = c.xA.ap(), c.xB.ap()
        if "w2p" in phases and "attn_a" not in phases:
            build_w2p(c)
        cur, cur_name = x0, "x0"
        if "attn_a" in phases:
            build_attn_a(c, cur, xA, "xA", with_w2p="w2p" in phases)
            cur, cur_name = xA, "xA"
        if "ffn1" in phases:
            dst, dname = (xB, "xB") if ("attn_b" in phases or "ffn2" in phases) \
                else (out, "out")
            build_ffn(c, cur, dst, dname, "1")
            cur, cur_name = dst, dname
        if "attn_b" in phases:
            dst, dname = (xA, "xA") if "ffn2" in phases else (out, "out")
            build_attn_b(c, cur, dst, dname)
            cur, cur_name = dst, dname
        if "ffn2" in phases:
            build_ffn(c, cur, out, "out", "2")
            cur, cur_name = out, "out"
        if cur_name != "out":
            p.dma("qsp", out, cur, slot="out", dst_key="out", src_key=cur_name)
        p.barrier_wait_all_dma()
        p.emit()
    nc.finalize()
    return nc


_DUMMY = None


def stage_shared_dummy():
    global _DUMMY
    if _DUMMY is None:
        w = {}
        rng = np.random.default_rng(0)
        w["L0_attn_a_w_qkv"] = np.zeros((6144, 2048), np.float32)
        w["L0_attn_a_b_qkv"] = np.zeros((6144,), np.float32)
        w["L0_attn_a_w_out"] = np.zeros((2048, 2048), np.float32)
        w["L0_attn_a_b_out"] = np.zeros((2048,), np.float32)
        w["L0_ffn_W1"] = np.zeros((8192, 2048), np.float32)
        w["L0_ffn_W2"] = np.zeros((2048, 8192), np.float32)
        w["L0_ffn_W3"] = np.zeros((8192, 2048), np.float32)
        w["L0_attn_b_W_in_states"] = np.zeros((4096, 2048), np.float32)
        w["L0_attn_b_W_in_gates"] = np.zeros((4096, 2048), np.float32)
        w["L0_attn_b_conv_w"] = np.zeros((4096, 4), np.float32)
        w["L0_attn_b_conv_b"] = np.zeros((4096,), np.float32)
        w["L0_attn_b_W_dt"] = np.zeros((128, 4096), np.float32)
        w["L0_attn_b_W_b"] = np.zeros((16, 4096), np.float32)
        w["L0_attn_b_W_c"] = np.zeros((16, 4096), np.float32)
        w["L0_attn_b_W_dtproj"] = np.zeros((4096, 128), np.float32)
        w["L0_attn_b_b_dtproj"] = np.zeros((4096,), np.float32)
        w["L0_attn_b_A_log"] = np.zeros((4096, 16), np.float32)
        w["L0_attn_b_D"] = np.zeros((4096,), np.float32)
        w["L0_attn_b_W_out"] = np.zeros((2048, 4096), np.float32)
        _DUMMY = stage_shared(w)
    return _DUMMY


def kernel(**inputs):
    x = np.asarray(inputs["L0_attn_a_x"])
    staged = stage_shared(inputs)
    nc = build_program(ALL_PHASES)
    in_maps = []
    for b in range(8):
        m = dict(staged)
        m["x0"] = np.ascontiguousarray(x[b], dtype=np.float32)
        in_maps.append(m)
    res = run_bass_kernel_spmd(nc, in_maps, core_ids=list(range(8)))
    out = np.stack([np.asarray(res.results[b]["out"]) for b in range(8)])
    return out.astype(np.float32, copy=False)


# version 13
# speedup vs baseline: 1.1109x; 1.0230x over previous
import math
from contextlib import ExitStack

import numpy as np

import concourse.bass as bass
import concourse.mybir as mybir
from concourse.alu_op_type import AluOpType
from concourse.alu_op_type import AluOpType as OP
from concourse.bass_utils import run_bass_kernel_spmd



F32 = mybir.dt.float32
BF16 = mybir.dt.bfloat16
AF = mybir.ActivationFunctionType

ENGINES = ("pe", "dve", "act", "pool", "sp")
Q_ENGINE = {"qsp": "sp", "qact": "act", "qpool": "pool"}


def _key_of(ap_or_handle):
    t = getattr(ap_or_handle, "tensor", ap_or_handle)
    return t.name


class Prog:
    def __init__(self, nc):
        self.nc = nc
        self._ctx = []
        self.sem = {}
        for name in ENGINES:
            self.sem[name] = self._enter(nc.semaphore(f"sem_{name}"))
        self.cnt = {n: 0 for n in ENGINES}
        self.slot_sem = {}
        self.slot_cnt = {}
        self.hwm = {}
        self.ops = {n: [] for n in ENGINES}
        self.last_write = {}
        self.readers = {}
        self.n_waits = 0
        self.n_ops = 0

    def _enter(self, cm):
        v = cm.__enter__()
        self._ctx.append(cm)
        return v

    def close(self):
        for cm in reversed(self._ctx):
            cm.__exit__(None, None, None)
        self._ctx = []

    def _slot(self, name):
        if name not in self.slot_sem:
            self.slot_sem[name] = self._enter(self.nc.semaphore(f"sem_{name}"))
            self.slot_cnt[name] = 0
        return self.slot_sem[name]

    def _wait(self, eng, src, val):
        if val <= 0:
            return
        if src == eng == "pe":
            return
        k = (eng, src)
        if self.hwm.get(k, 0) >= val:
            return
        self.hwm[k] = val
        sem = self.sem[src] if src in ENGINES else self.slot_sem[src]
        self.ops[eng].append(("w", lambda e, sem=sem, val=val: e.wait_ge(sem, val)))
        self.n_waits += 1

    def _read_deps(self, eng, keys):
        for k in keys:
            lw = self.last_write.get(k)
            if lw is not None:
                self._wait(eng, lw[0], lw[1])

    def _write_deps(self, eng, keys):
        for k in keys:
            lw = self.last_write.get(k)
            if lw is not None:
                self._wait(eng, lw[0], lw[1])
            for src, v in self.readers.get(k, {}).items():
                self._wait(eng, src, v)

    def _record_read(self, src, cnt, keys):
        for k in keys:
            self.readers.setdefault(k, {})
            prev = self.readers[k].get(src, 0)
            self.readers[k][src] = max(prev, cnt)

    def _record_write(self, src, cnt, keys):
        for k in keys:
            self.last_write[k] = (src, cnt)
            self.readers[k] = {}

    def op(self, eng, fn, outs, ins, inc_extra=0, name=None):
        okeys = [x if isinstance(x, str) else _key_of(x) for x in outs]
        ikeys = [x if isinstance(x, str) else _key_of(x) for x in ins]
        self._read_deps(eng, ikeys)
        self._write_deps(eng, okeys)
        self.cnt[eng] += 1
        cnt = self.cnt[eng]
        sem = self.sem[eng]

        def run(e, fn=fn, sem=sem):
            inst = fn(e)
            inst.then_inc(sem, 1)
        self.ops[eng].append(("o", run))
        self.n_ops += 1
        self._record_read(eng, cnt, ikeys)
        self._record_write(eng, cnt, okeys)
        return cnt

    def group(self, eng, fns, outs, ins):
        okeys = [x if isinstance(x, str) else _key_of(x) for x in outs]
        ikeys = [x if isinstance(x, str) else _key_of(x) for x in ins]
        self._read_deps(eng, ikeys)
        self._write_deps(eng, okeys)
        self.cnt[eng] += 1
        cnt = self.cnt[eng]
        sem = self.sem[eng]

        def run(e, fns=fns, sem=sem):
            inst = None
            for f in fns:
                inst = f(e)
            inst.then_inc(sem, 1)
        self.ops[eng].append(("o", run))
        self.n_ops += len(fns)
        self._record_read(eng, cnt, ikeys)
        self._record_write(eng, cnt, okeys)
        return cnt

    def dma(self, queue, dst, src, slot=None, dst_key=None, src_key=None):
        eng = Q_ENGINE[queue]
        dk = [dst_key or _key_of(dst)]
        sk = [src_key or _key_of(src)]
        slot = slot or dk[0]
        sem = self._slot(slot)
        self._read_deps(eng, sk)
        self._write_deps(eng, dk)
        self.slot_cnt[slot] += 16
        cnt = self.slot_cnt[slot]

        def run(e, dst=dst, src=src, sem=sem):
            e.dma_start(dst, src).then_inc(sem, 16)
        self.ops[eng].append(("o", run))
        self.n_ops += 1
        self._record_read(slot, cnt, sk)
        self._record_write(slot, cnt, dk)
        return cnt

    def barrier_wait_all_dma(self):
        for s, c in self.slot_cnt.items():
            if c:
                self._wait("sp", s, c)

    def fence(self):
        srcs = ([(e, self.cnt[e]) for e in ENGINES] +
                [(s, c) for s, c in self.slot_cnt.items()])
        for eng in ENGINES:
            for src, val in srcs:
                if val > 0:
                    self._wait(eng, src, val)

    def emit(self):
        nc = self.nc
        with nc.Block() as blk:
            for eng, attr in (("sp", "sync"), ("pe", "tensor"), ("dve", "vector"),
                              ("act", "scalar"), ("pool", "gpsimd")):
                lst = self.ops[eng]
                if not lst:
                    continue
                def body(e, lst=lst):
                    for kind, fn in lst:
                        fn(e)
                getattr(blk, attr)(body)



def tt(p, out, a, b, op, okey=None, akey=None, bkey=None):
    p.op("dve", lambda e: e.tensor_tensor(out, a, b, op),
         [okey or out], [akey or a, bkey or b])


def tt_pool(p, out, a, b, op):
    p.op("pool", lambda e: e.tensor_tensor(out, a, b, op), [out], [a, b])


def ts(p, out, in0, s1, s2, op0, op1=AluOpType.bypass, okey=None, ikey=None):
    ins = [ikey or in0]
    for s in (s1, s2):
        if isinstance(s, bass.AP):
            ins.append(s)
    p.op("dve", lambda e: e.tensor_scalar(out, in0, s1, s2, op0, op1),
         [okey or out], ins)


def stt(p, out, in0, scalar, in1, op0, op1, okey=None):
    ins = [in0, in1]
    if isinstance(scalar, bass.AP):
        ins.append(scalar)
    p.op("dve", lambda e: e.scalar_tensor_tensor(out, in0, scalar, in1, op0, op1),
         [okey or out], ins)


def act(p, out, in_, func, bias=None, scale=None, okey=None, ikey=None,
        accum_out=None):
    kw = {}
    ins = [ikey or in_]
    if bias is not None:
        kw["bias"] = bias
        if isinstance(bias, bass.AP):
            ins.append(bias)
    if scale is not None:
        kw["scale"] = scale
        if isinstance(scale, bass.AP):
            ins.append(scale)
    outs = [okey or out]
    if accum_out is not None:
        kw["accum_out"] = accum_out
        outs.append(accum_out)
    p.op("act", lambda e: e.activation(out, in_, func, **kw), outs, ins)


def act_copy(p, out, in_, okey=None, ikey=None, scale=None, bias=None):
    act(p, out, in_, AF.Copy if bias is None else AF.Identity,
        bias=bias, scale=scale, okey=okey, ikey=ikey)


def dve_copy(p, out, in_, okey=None, ikey=None):
    p.op("dve", lambda e: e.tensor_copy(out, in_), [okey or out], [ikey or in_])


def scan(p, out, d0, d1, initial, okey=None, keys=None):
    ins = [d0, d1] if keys is None else keys
    if isinstance(initial, bass.AP):
        ins = list(ins) + [initial]
    p.op("dve", lambda e: e.tensor_tensor_scan(out, d0, d1, initial,
                                               AluOpType.mult, AluOpType.add),
         [okey or out], ins)


def mm(p, out, lhsT, rhs, start=True, stop=True, okey=None, lkey=None, rkey=None):
    p.op("pe", lambda e: e.matmul(out, lhsT, rhs, start=start, stop=stop),
         [okey or out], [lkey or lhsT, rkey or rhs])


def mm_group(p, out, pairs, okey=None, ikeys=None, start=True, stop=True):
    fns = []
    n = len(pairs)
    for i, (l, r) in enumerate(pairs):
        fns.append(lambda e, l=l, r=r, s=(start and i == 0),
                   t=(stop and i == n - 1):
                   e.matmul(out, l, r, start=s, stop=t))
    ins = ikeys if ikeys is not None else [x for pr in pairs for x in pr]
    p.group("pe", fns, [okey or out], ins)


def transpose(p, out, in_, ident, okey=None, ikey=None):
    p.op("pe", lambda e: e.transpose(out, in_, ident),
         [okey or out], [ikey or in_, ident])


def memset(p, eng, out, val, okey=None):
    p.op(eng, lambda e: e.memset(out, val), [okey or out], [])


"""Megakernel builder for mk_hybrid_d1: full forward of
  x += attn_a(LN(x)); x += ffn(LN(x)); x += ssm(LN(x)); x += ffn(LN(x))
on one NeuronCore per batch element (8 cores, data-parallel over batch).

Layouts:
  residual x: DRAM (T=2048 rows t, D=2048 cols d) fp32, streamed in 128-row tiles
  ln_dm: SBUF (128 p=d%128, 16 dblk, T) bf16  (feature-major for matmuls)
  weights: host pre-tiled to (loop_idx, partition, free...) fp32; DMA-cast bf16
  scratch (per core, Internal DRAM): q_s/k_s (dmblk,p,t) bf16, v_s (tblk,p,dv),
  hid_s (hblk,p,t), w2p_s (p,B,o) [FWHT-folded W2'], hs_s/dt_s/sg_s/y_s
  (db,p,t) bf16, xA/xB fp32, bcflat_s.
"""


T = 2048
D = 2048
NH, HD = 16, 128
FH = 8192
DI = 4096
NS = 16
RK = 128
SQRT2 = 1.4142135623730951

ALL_PHASES = ("w2p", "attn_a", "ffn1", "attn_b", "ffn2")


def _slopes(n):
    start = 2.0 ** (-2.0 ** (-(math.log2(n) - 3)))
    return np.array([start ** (i + 1) for i in range(n)])


def _had(n):
    i = np.arange(n)
    H = np.empty((n, n), np.float32)
    for a in range(n):
        H[a] = np.array([(-1.0) ** bin(a & j).count("1") for j in i], np.float32)
    return H


def stage_shared(w):
    f = np.float32
    s = {}
    A = np.ascontiguousarray
    wqkv = w["L0_attn_a_w_qkv"]
    wqkT = wqkv.T
    qk = wqkT[:, :4096].reshape(16, 128, 32, 128)
    s["wqkT"] = A(qk.transpose(2, 1, 0, 3)).astype(f)
    vv = wqkT[:, 4096:].reshape(16, 128, 4, 512)
    s["wvT"] = A(vv.transpose(2, 1, 0, 3)).astype(f)
    bqkv = w["L0_attn_a_b_qkv"]
    s["bqk_cols"] = A(bqkv[:4096].reshape(32, 128).T).astype(f)
    s["bv_rep"] = A(np.broadcast_to(bqkv[4096:], (128, 2048))).astype(f)
    s["woutT_t"] = A(w["L0_attn_a_w_out"].T.reshape(16, 128, 2048)
                     .transpose(1, 0, 2)).astype(f)
    s["bout_rep"] = A(np.broadcast_to(w["L0_attn_a_b_out"], (128, 2048))).astype(f)
    w1T = w["L0_ffn_W1"].T
    s["w1T_t"] = A(w1T.reshape(16, 128, 64, 128).transpose(2, 1, 0, 3)).astype(f)
    w3T = w["L0_ffn_W3"].T
    s["w3T_t"] = A(w3T.reshape(16, 128, 64, 128).transpose(2, 1, 0, 3)).astype(f)
    w2T = w["L0_ffn_W2"].T
    s["w2T_t"] = A(w2T.reshape(64, 128, 16, 128).transpose(2, 1, 0, 3)).astype(f)
    s["winsT_t"] = A(w["L0_attn_b_W_in_states"].T.reshape(16, 128, 32, 128)
                     .transpose(2, 1, 0, 3)).astype(f)
    s["wingT_t"] = A(w["L0_attn_b_W_in_gates"].T.reshape(16, 128, 32, 128)
                     .transpose(2, 1, 0, 3)).astype(f)
    s["convw_cols"] = A(w["L0_attn_b_conv_w"].reshape(32, 128, 4)
                        .transpose(1, 0, 2).reshape(128, 128)).astype(f)
    s["convb_cols"] = A(w["L0_attn_b_conv_b"].reshape(32, 128).T).astype(f)
    s["wdtT_t"] = A(w["L0_attn_b_W_dt"].T.reshape(32, 128, 128)
                    .transpose(1, 0, 2)).astype(f)
    s["wdtpT_t"] = A(w["L0_attn_b_W_dtproj"].T.reshape(128, 32, 128)).astype(f)
    s["bdtp_cols"] = A(w["L0_attn_b_b_dtproj"].reshape(32, 128).T).astype(f)
    wbc = np.concatenate([w["L0_attn_b_W_b"], w["L0_attn_b_W_c"]], 0)
    s["wbcT_t"] = A(wbc.T.reshape(32, 128, 32).transpose(1, 0, 2)).astype(f)
    s["alog_cols"] = A(w["L0_attn_b_A_log"].reshape(32, 128, 16)
                       .transpose(1, 0, 2).reshape(128, 512)).astype(f)
    s["d_cols"] = A(w["L0_attn_b_D"].reshape(32, 128).T).astype(f)
    s["woutbT_t"] = A(w["L0_attn_b_W_out"].T.reshape(32, 128, 2048)
                      .transpose(1, 0, 2)).astype(f)
    lam = np.exp(-_slopes(NH))
    pos = np.arange(128, dtype=np.float64)
    dq = lam[:, None] ** (pos + 1.0)
    dk = lam[:, None] ** (127.0 - pos)
    diff = pos[:, None] - pos[None, :]
    dm = np.where(diff >= 0, lam[:, None, None] ** np.maximum(diff, 0), 0.0)
    s["dq_rep"] = A(np.broadcast_to(dq[None], (128, NH, 128))).astype(f)
    s["maskT"] = A(dm.transpose(2, 0, 1)).astype(f)
    s["dk_cols"] = A(dk.T).astype(f)
    s["h128"] = _had(128) * (2.0 ** -7)
    s["ident"] = np.eye(128, dtype=f)
    s["ones1"] = np.ones((1, 128), f)
    return s


def lam_c_floats():
    lam = np.exp(-_slopes(NH))
    return [float(x) for x in lam ** 128]


class C:
    def __init__(self, nc, p, stack, io):
        self.nc, self.p, self.stack, self.io = nc, p, stack, io
        self.sb_cache = {}

    def sb(self, name, shape, dt=F32):
        t = self.stack.enter_context(self.nc.sbuf_tensor(name, list(shape), dt))
        return t

    def ps(self, name, shape, dt=F32):
        return self.stack.enter_context(self.nc.psum_tensor(name, list(shape), dt))

    def dram(self, name, shape, dt, kind="Internal"):
        return self.nc.dram_tensor(name, list(shape), dt, kind=kind)


def build_init(c):
    p, io = c.p, c.io
    for name, shape, dt, src in [
        ("ident_f", (128, 128), F32, "ident"),
        ("ident_b", (128, 128), BF16, "ident"),
        ("h128_b", (128, 128), BF16, "h128"),
        ("bqk_cols", (128, 32), F32, "bqk_cols"),
        ("bv_rep", (128, 2048), BF16, "bv_rep"),
        ("bout_rep", (128, 2048), BF16, "bout_rep"),
        ("dq_rep", (128, NH, 128), BF16, "dq_rep"),
        ("maskT", (128, NH, 128), BF16, "maskT"),
        ("dk_cols", (128, NH), F32, "dk_cols"),
        ("convw_cols", (128, 128), F32, "convw_cols"),
        ("convb_cols", (128, 32), F32, "convb_cols"),
        ("bdtp_cols", (128, 32), F32, "bdtp_cols"),
        ("alog_cols", (128, 512), F32, "alog_cols"),
        ("d_cols", (128, 32), F32, "d_cols"),
    ]:
        t = c.sb("s_" + name, shape, dt)
        setattr(c, name, t)
        p.dma("qsp" if dt == F32 else "qpool", t[:], io[src])
    c.ones1 = c.sb("s_ones1", (1, 128), F32)
    p.dma("qsp", c.ones1[:], io["ones1"])
    c.c_eps = c.sb("c_eps", (128, 1), F32)
    memset(p, "pool", c.c_eps[:], 1e-5)
    c.c_one = c.sb("c_one", (128, 1), F32)
    memset(p, "pool", c.c_one[:], 1.0)
    c.aneg = c.sb("aneg", (128, 512), F32)
    act(p, c.aneg[:], c.alog_cols[:], AF.Exp)
    ts(p, c.aneg[:], c.aneg[:], -1.0, None, OP.mult)
    c.psA = c.ps("psA", (128, 2048), F32)
    c.psB = c.ps("psB", (128, 1024), F32)
    c.ptr = [c.ps("ptr0", (128, 128), BF16), c.ps("ptr1", (128, 128), BF16)]
    c.q_s = c.dram("q_s", (16, 128, T), BF16)
    c.k_s = c.dram("k_s", (16, 128, T), BF16)
    c.v_s = c.dram("v_s", (16, 128, D), BF16)
    c.hid_s = c.dram("hid_s", (64, 128, T), BF16)
    c.w2p_s = c.dram("w2p_s", (128, 64, 2048), BF16)
    c.hs_s = c.dram("hs_s", (32, 128, T), BF16)
    c.dt_s = c.dram("dt_s", (32, 128, T), BF16)
    c.sg_s = c.dram("sg_s", (32, 128, T), BF16)
    c.y_s = c.dram("y_s", (32, 128, T), BF16)
    c.bcflat_s = c.dram("bcflat_s", (2, 16, T), F32)
    c.xA = c.dram("xA", (T, D), F32)
    c.xB = c.dram("xB", (T, D), F32)


def psA(c, i, w=512):
    return c.psA[:, i * 512:i * 512 + w], f"psA{i}"


def psB(c, i, w=512):
    return c.psB[:, i * 512:i * 512 + w], f"psB{i}"


def gen_w2p(c, st):
    p, io = c.p, c.io
    OW = 128
    if True:
        wch = [st.enter_context(c.nc.sbuf_tensor("w2c0", [128, 64, OW], BF16))]
        fa = st.enter_context(c.nc.sbuf_tensor("w2fa", [128, 64, OW], BF16))
        fb = st.enter_context(c.nc.sbuf_tensor("w2fb", [128, 64, OW], BF16))
        outb = st.enter_context(c.nc.sbuf_tensor("w2ob", [128, 64, OW], BF16))
        for och in range(16):
            wc = wch[0]
            p.dma("qpool", wc[:], io["w2T_t"][och], slot="w2c0")
            beng = "dve"
            for B in range(64):
                pa = c.psA[:, (2 + B % 2) * 512:(2 + B % 2) * 512 + OW]
                key = f"hb{2 + B % 2}"
                mm(p, pa, c.h128_b[:], wc[:, B, :], okey=key,
                   lkey="h128_b", rkey="w2c0")
                if och % 2 == 0:
                    p.op("dve", lambda e, o=fa[:, B, :], i_=pa:
                         e.tensor_scalar(o, i_, SQRT2, None, OP.mult),
                         ["w2fa"], [key])
                else:
                    act(p, fa[:, B, :], pa, AF.Copy, scale=SQRT2,
                        okey="w2fa", ikey=key)
            bufs = [fa, fb]
            cur = 0
            for si, h in enumerate([1, 2, 4, 8, 16, 32]):
                src, dst = bufs[cur], bufs[1 - cur]
                last = si == 5
                if last:
                    dst = outb
                sap = src[:].rearrange("p (g x j) o -> p g x j o", x=2, j=h)
                dap = dst[:].rearrange("p (g x j) o -> p g x j o", x=2, j=h)
                p.op(beng, lambda e, o=dap[:, :, 0], a=sap[:, :, 0],
                     b=sap[:, :, 1]: e.tensor_tensor(o, a, b, OP.add),
                     [dst.name], [src.name])
                p.op(beng, lambda e, o=dap[:, :, 1], a=sap[:, :, 0],
                     b=sap[:, :, 1]: e.tensor_tensor(o, a, b, OP.subtract),
                     [dst.name], [src.name])
                cur = 1 - cur
            for bb in range(0, 64, 8):
                p.dma("qact", c.w2p_s.ap()[:, bb:bb + 8, och * OW:(och + 1) * OW],
                      outb[:, bb:bb + 8, :], slot="w2p_s", dst_key="w2p_s",
                      src_key="w2ob")
            yield


def build_w2p(c):
    with ExitStack() as st:
        for _ in gen_w2p(c, st):
            pass
        c.p.fence()


def emit_ln_dm(c, x_src, ln_dm, tag):
    p = c.p
    with ExitStack() as st:
        xt = [st.enter_context(c.nc.sbuf_tensor(f"lnx{tag}{i}", [128, D], F32))
              for i in range(2)]
        lnt = [st.enter_context(c.nc.sbuf_tensor(f"lnt{tag}{i}", [128, D], BF16))
               for i in range(2)]
        st6 = [st.enter_context(c.nc.sbuf_tensor(f"st6{tag}{i}", [128, 4, 6], F32))
               for i in range(2)]
        st2 = [st.enter_context(c.nc.sbuf_tensor(f"st2{tag}{i}", [128, 2], F32))
               for i in range(2)]
        ngm = [st.enter_context(c.nc.sbuf_tensor(f"ngm{tag}{i}", [128, 1], F32))
               for i in range(2)]
        rsd = [st.enter_context(c.nc.sbuf_tensor(f"rsd{tag}{i}", [128, 1], F32))
               for i in range(2)]
        xname = x_src.tensor.name
        p.dma("qsp", xt[0][:], x_src[0:128, :], slot=f"lnx{tag}0",
              src_key=f"{xname}.0")
        for tb in range(16):
            i = tb % 2
            if tb < 15:
                ii = (tb + 1) % 2
                p.dma("qsp", xt[ii][:],
                      x_src[(tb + 1) * 128:(tb + 2) * 128, :],
                      slot=f"lnx{tag}{ii}", src_key=f"{xname}.{(tb + 1) // 4}")
            for k in range(4):
                p.op("dve", lambda e, a=st6[i], b=xt[i], k=k:
                     e.bn_stats(a[:, k, :], b[:, k * 512:(k + 1) * 512]),
                     [st6[i]], [xt[i]])
            p.op("dve", lambda e, a=st2[i], b=st6[i]:
                 e.bn_aggr(a[:], b[:].rearrange("p a b -> p (a b)")),
                 [st2[i]], [st6[i]])
            ts(p, ngm[i][:], st2[i][:, 0:1], -1.0, None, OP.mult,
               okey=ngm[i].name, ikey=st2[i].name)
            act(p, rsd[i][:], st2[i][:, 1:2], AF.Sqrt, bias=c.c_eps[:],
                okey=rsd[i].name, ikey=st2[i].name)
            p.op("dve", lambda e, a=rsd[i]: e.reciprocal(a[:], a[:]),
                 [rsd[i]], [rsd[i]])
            ts(p, lnt[i][:], xt[i][:], ngm[i][:], rsd[i][:], OP.add, OP.mult)
            for db in range(16):
                tr = c.ptr[db % 2]
                transpose(p, tr[:], lnt[i][:, db * 128:(db + 1) * 128],
                          c.ident_b[:], okey=tr.name, ikey=lnt[i].name)
                act_copy(p, ln_dm[:, db, tb * 128:(tb + 1) * 128], tr[:],
                         okey=f"{ln_dm.name}.{tb // 4}", ikey=tr.name)
        p.fence()


def build_attn_a(c, x_src, x_dst, x_dst_name, with_w2p=False):
    p, io = c.p, c.io
    lam_c = lam_c_floats()
    with ExitStack() as st:
        ln_dm = st.enter_context(c.nc.sbuf_tensor("ln_a", [128, 16, T], BF16))
        emit_ln_dm(c, x_src, ln_dm, "a")
        with ExitStack() as st2:
            w2st = ExitStack()
            w2gen = gen_w2p(c, w2st) if with_w2p else None
            wqk = [st2.enter_context(c.nc.sbuf_tensor(f"wqk{i}", [128, 16, 128],
                                                      BF16)) for i in range(2)]
            qkt = [st2.enter_context(c.nc.sbuf_tensor(f"qkt{i}", [128, 512],
                                                      BF16)) for i in range(2)]
            p.dma("qpool", wqk[0][:], io["wqkT"][0], slot=wqk[0].name)
            for ms in range(32):
                if w2gen is not None:
                    next(w2gen, None)
                if ms < 31:
                    wn = wqk[(ms + 1) % 2]
                    p.dma("qpool", wn[:], io["wqkT"][ms + 1], slot=wn.name)
                wb = wqk[ms % 2]
                dst = c.q_s if ms < 16 else c.k_s
                blk = ms % 16
                for tn in range(4):
                    j4 = (ms * 4 + tn) % 4
                    j = (ms * 4 + tn) % 2
                    pb, key = psB(c, j4) if j4 < 2 else psA(c, j4 - 2)
                    mm_group(p, pb, [(wb[:, kb, :],
                                      ln_dm[:, kb, tn * 512:(tn + 1) * 512])
                                     for kb in range(16)],
                             okey=key, ikeys=[wb.name, f"ln_a.{tn}"])
                    act(p, qkt[j][:], pb, AF.Identity,
                        bias=c.bqk_cols[:, ms:ms + 1], okey=qkt[j].name, ikey=key)
                    p.dma("qact", dst.ap()[blk, :, tn * 512:(tn + 1) * 512],
                          qkt[j][:], slot=dst.name, dst_key=dst.name,
                          src_key=qkt[j].name)
            if w2gen is not None:
                for _ in w2gen:
                    pass
                w2st.close()
                p.fence()
            wv = [st2.enter_context(c.nc.sbuf_tensor(f"wv{i}", [128, 16, 512],
                                                     BF16)) for i in range(2)]
            vt = [st2.enter_context(c.nc.sbuf_tensor(f"vt{i}", [128, 512], BF16))
                  for i in range(2)]
            p.dma("qpool", wv[0][:], io["wvT"][0], slot=wv[0].name)
            for dv in range(4):
                if dv < 3:
                    wn = wv[(dv + 1) % 2]
                    p.dma("qpool", wn[:], io["wvT"][dv + 1], slot=wn.name)
                wb = wv[dv % 2]
                for tsl in range(16):
                    j4 = (dv * 16 + tsl) % 4
                    j = (dv * 16 + tsl) % 2
                    pb, key = psB(c, j4) if j4 < 2 else psA(c, j4 - 2)
                    mm_group(p, pb, [(ln_dm[:, kb, tsl * 128:(tsl + 1) * 128],
                                      wb[:, kb, :]) for kb in range(16)],
                             okey=key, ikeys=[wb.name, f"ln_a.{tsl // 4}"])
                    tt(p, vt[j][:], pb, c.bv_rep[:, dv * 512:(dv + 1) * 512],
                       OP.add, okey=vt[j].name, akey=key, bkey="bv_rep")
                    p.dma("qact", c.v_s.ap()[tsl, :, dv * 512:(dv + 1) * 512],
                          vt[j][:], slot="v_s", dst_key="v_s",
                          src_key=vt[j].name)
            p.fence()
    with ExitStack() as st:
        nsb = c.nc.sbuf_tensor
        wout = st.enter_context(nsb("wout_r", [128, 16, 2048], BF16))
        for i in range(4):
            p.dma("qpool", wout[:, i * 4:(i + 1) * 4, :],
                  io["woutT_t"][:, i * 4:(i + 1) * 4, :], slot="wout_r",
                  dst_key="wout_r")
        qc = st.enter_context(nsb("qc", [128, 16, 512], BF16))
        kc = st.enter_context(nsb("kc", [128, 16, 512], BF16))
        vc = st.enter_context(nsb("vc", [128, 4, 2048], BF16))
        S_f = st.enter_context(nsb("S_f", [128, NH, 128], F32))
        S_b = st.enter_context(nsb("S_b", [128, NH, 128], BF16))
        Stmp = [st.enter_context(nsb(f"Stmp{i}", [128, 128], F32)) for i in range(4)]
        smask = [st.enter_context(nsb(f"smask{i}", [128, 128], BF16)) for i in range(4)]
        ktdk = [st.enter_context(nsb(f"ktdk{i}", [128, 128], BF16)) for i in range(4)]
        qdq = [st.enter_context(nsb(f"qdq{i}", [128, 128], BF16)) for i in range(4)]
        attn = st.enter_context(nsb("attn_dm", [128, 16, 512], BF16))
        xt = st.enter_context(nsb("xta", [128, 2048], F32))
        xnew = st.enter_context(nsb("xna", [128, 2048], F32))
        p.op("dve", lambda e: e.memset(S_f[:], 0.0),
             [f"S_f.{h}" for h in range(NH)], [])
        p.op("dve", lambda e: e.memset(S_b[:], 0.0),
             [f"S_b.{h}" for h in range(NH)], [])
        for tch in range(4):
            sl = slice(tch * 512, (tch + 1) * 512)
            for g in range(4):
                p.dma("qsp", qc[:, g * 4:(g + 1) * 4, :],
                      c.q_s.ap()[g * 4:(g + 1) * 4, :, sl]
                      .rearrange("b p t -> p b t"), slot="qc", dst_key="qc",
                      src_key="q_s")
                p.dma("qsp", kc[:, g * 4:(g + 1) * 4, :],
                      c.k_s.ap()[g * 4:(g + 1) * 4, :, sl]
                      .rearrange("b p t -> p b t"), slot="kc", dst_key="kc",
                      src_key="k_s")
            for i in range(4):
                p.dma("qsp", vc[:, i, :], c.v_s.ap()[tch * 4 + i],
                      slot="vc", dst_key="vc", src_key="v_s")
            for lc in range(4):
                cs = slice(lc * 128, (lc + 1) * 128)
                def hbank(b):
                    if b < 4:
                        return c.psA[:, b * 512:b * 512 + 128], f"hb{b}"
                    return c.psB[:, (b - 4) * 512:(b - 4) * 512 + 128], f"hb{b}"

                def stage1(h):
                    b = h % 6
                    pbk, key = hbank(b)
                    mm(p, pbk, kc[:, h, cs], qc[:, h, cs], okey=key,
                       lkey="kc", rkey="qc")
                    tr = c.ptr[h % 2]
                    transpose(p, tr[:], kc[:, h, cs], c.ident_b[:],
                              okey=f"ptr{h % 2}", ikey="kc")

                def stage2(h):
                    b = h % 6
                    pbk, key = hbank(b)
                    tt(p, smask[h % 4][:], pbk, c.maskT[:, h, :], OP.mult,
                       okey=smask[h % 4].name, akey=key, bkey="maskT")
                    ts(p, ktdk[h % 4][:], c.ptr[h % 2][:],
                       c.dk_cols[:, h:h + 1], None, OP.mult,
                       okey=ktdk[h % 4].name, ikey=f"ptr{h % 2}")
                    tt(p, qdq[h % 4][:], qc[:, h, cs], c.dq_rep[:, h, :],
                       OP.mult, okey=qdq[h % 4].name, akey="qc", bkey="dq_rep")

                def stage3(h):
                    b = h % 6
                    pbk, key = hbank(b)
                    v_t = vc[:, lc, h * 128:(h + 1) * 128]
                    mm(p, pbk, S_b[:, h, :], qdq[h % 4][:],
                       start=True, stop=False, okey=key, lkey=f"S_b.{h}",
                       rkey=qdq[h % 4].name)
                    mm(p, pbk, v_t, smask[h % 4][:], start=False, stop=True,
                       okey=key, lkey="vc", rkey=smask[h % 4].name)

                def stage4(h):
                    b = h % 6
                    pbk, key = hbank(b)
                    act_copy(p, attn[:, h, cs], pbk, okey=f"attn.{lc}",
                             ikey=key)

                def stage5(h):
                    b = h % 6
                    pbk, key = hbank(b)
                    v_t = vc[:, lc, h * 128:(h + 1) * 128]
                    mm(p, pbk, ktdk[h % 4][:], v_t, okey=key,
                       lkey=ktdk[h % 4].name, rkey="vc")
                    ts(p, Stmp[h % 4][:], S_f[:, h, :], lam_c[h], None,
                       OP.mult, okey=Stmp[h % 4].name, ikey=f"S_f.{h}")
                    tt(p, S_f[:, h, :], Stmp[h % 4][:], pbk, OP.add,
                       okey=f"S_f.{h}", akey=Stmp[h % 4].name, bkey=key)
                    act_copy(p, S_b[:, h, :], S_f[:, h, :], okey=f"S_b.{h}",
                             ikey=f"S_f.{h}")

                stage1(0)
                stage2(0)
                for h in range(NH):
                    if h + 1 < NH:
                        stage1(h + 1)
                        stage2(h + 1)
                    stage3(h)
                    stage4(h)
                    stage5(h)
            for tsl in range(4):
                row = tch * 512 + tsl * 128
                p.dma("qsp", xt[:], x_src[row:row + 128, :], slot="xta",
                      src_key=f"{x_src.tensor.name}.{row // 512}")
                for oc in range(4):
                    if oc % 2 == 0:
                        pb = c.psB[:, 512:1024]
                        keys = ["hb5"]
                    else:
                        pb = c.psA[:, 1536:2048]
                        keys = ["hb3"]
                    fns = []
                    for db in range(16):
                        fns.append(lambda e, o=pb, l=attn[:, db, tsl * 128:
                                                         (tsl + 1) * 128],
                                   r=wout[:, db, oc * 512:(oc + 1) * 512],
                                   s=(db == 0), t=(db == 15):
                                   e.matmul(o, l, r, start=s, stop=t))
                    p.group("pe", fns, keys, [f"attn.{tsl}", "wout_r"])
                    osl = slice(oc * 512, (oc + 1) * 512)
                    tt(p, xnew[:, osl], pb, c.bout_rep[:, osl], OP.add,
                       okey="xna", akey=keys[0], bkey="bout_rep")
                    tt(p, xnew[:, osl], xnew[:, osl], xt[:, osl], OP.add,
                       okey="xna", akey="xna", bkey="xta")
                p.dma("qact", x_dst[row:row + 128, :], xnew[:],
                      slot=f"{x_dst_name}.{row // 512}",
                      dst_key=f"{x_dst_name}.{row // 512}", src_key="xna")
        p.fence()


def build_ffn(c, x_src, x_dst, x_dst_name, tag):
    p, io = c.p, c.io
    with ExitStack() as st:
        nsb = c.nc.sbuf_tensor
        ln_dm = st.enter_context(nsb(f"ln_f{tag}", [128, 16, T], BF16))
        emit_ln_dm(c, x_src, ln_dm, f"f{tag}")
        with ExitStack() as st2:
            w1b = [st2.enter_context(nsb(f"w1b{tag}{i}", [128, 16, 128], BF16))
                   for i in range(2)]
            w3b = [st2.enter_context(nsb(f"w3b{tag}{i}", [128, 16, 128], BF16))
                   for i in range(2)]
            gt = [st2.enter_context(nsb(f"gt{tag}{i}", [128, 512], F32))
                  for i in range(2)]
            ht = [st2.enter_context(nsb(f"ht{tag}{i}", [128, 512], BF16))
                  for i in range(2)]
            p.dma("qpool", w1b[0][:], io["w1T_t"][0], slot=w1b[0].name)
            p.dma("qpool", w3b[0][:], io["w3T_t"][0], slot=w3b[0].name)
            for hs_ in range(64):
                i = hs_ % 2
                if hs_ < 63:
                    ii = (hs_ + 1) % 2
                    p.dma("qpool", w1b[ii][:], io["w1T_t"][hs_ + 1],
                          slot=w1b[ii].name)
                    p.dma("qpool", w3b[ii][:], io["w3T_t"][hs_ + 1],
                          slot=w3b[ii].name)
                for tn in range(4):
                    tsl = slice(tn * 512, (tn + 1) * 512)
                    j = tn % 2
                    g_ps, g_key = psA(c, j)
                    l_ps, l_key = psA(c, 2 + j)
                    mm_group(p, g_ps, [(w1b[i][:, kb, :], ln_dm[:, kb, tsl])
                                       for kb in range(16)],
                             okey=g_key, ikeys=[w1b[i].name, f"{ln_dm.name}.{tn}"])
                    mm_group(p, l_ps, [(w3b[i][:, kb, :], ln_dm[:, kb, tsl])
                                       for kb in range(16)],
                             okey=l_key, ikeys=[w3b[i].name, f"{ln_dm.name}.{tn}"])
                    act(p, gt[j][:], g_ps, AF.Silu, okey=gt[j].name, ikey=g_key)
                    tt(p, ht[j][:], gt[j][:], l_ps, OP.mult, okey=ht[j].name,
                       akey=gt[j].name, bkey=l_key)
                    p.dma("qact", c.hid_s.ap()[hs_, :, tsl], ht[j][:],
                          slot="hid_s", dst_key="hid_s", src_key=ht[j].name)
            p.fence()
    with ExitStack() as st:
        nsb = c.nc.sbuf_tensor
        w2po = st.enter_context(nsb(f"w2po{tag}", [128, 64, 512], BF16))
        hidb = [st.enter_context(nsb(f"hidb{tag}{i}", [128, 64, 256], BF16))
                for i in range(2)]
        xoc = [st.enter_context(nsb(f"xoc{tag}{i}", [128, 512], F32)) for i in range(2)]
        xpt = [st.enter_context(nsb(f"xpt{tag}{i}", [128, 512], F32)) for i in range(2)]
        for oc in range(4):
            for bb in range(0, 64, 8):
                p.dma("qsp", w2po[:, bb:bb + 8, :],
                      c.w2p_s.ap()[:, bb:bb + 8, oc * 512:(oc + 1) * 512],
                      slot=w2po.name, dst_key=w2po.name, src_key="w2p_s")
            for tch in range(8):
                i = tch % 2
                for bb in range(0, 64, 16):
                    p.dma("qsp", hidb[i][:, bb:bb + 16, :],
                          c.hid_s.ap()[bb:bb + 16, :,
                                       tch * 256:(tch + 1) * 256]
                          .rearrange("b p t -> p b t"),
                          slot=hidb[i].name, dst_key=hidb[i].name,
                          src_key="hid_s")
                for tsl in range(2):
                    j = (tch * 2 + tsl) % 2
                    row = tch * 256 + tsl * 128
                    p.dma("qsp", xoc[j][:], x_src[row:row + 128,
                                                  oc * 512:(oc + 1) * 512],
                          slot=xoc[j].name,
                          src_key=f"{x_src.tensor.name}.{row // 512}")
                    pb, key = psB(c, j)
                    mm_group(p, pb, [(hidb[i][:, hb, tsl * 128:(tsl + 1) * 128],
                                      w2po[:, hb, :]) for hb in range(64)],
                             okey=key, ikeys=[hidb[i].name, w2po.name])
                    tt(p, xpt[j][:], pb, xoc[j][:], OP.add, okey=xpt[j].name,
                       akey=key, bkey=xoc[j].name)
                    p.dma("qact", x_dst[row:row + 128, oc * 512:(oc + 1) * 512],
                          xpt[j][:], slot=f"{x_dst_name}.{row // 512}",
                          dst_key=f"{x_dst_name}.{row // 512}",
                          src_key=xpt[j].name)
        p.fence()


def build_attn_b(c, x_src, x_dst, x_dst_name):
    p, io = c.p, c.io
    with ExitStack() as st:
        nsb = c.nc.sbuf_tensor
        ln_dm = st.enter_context(nsb("ln_b", [128, 16, T], BF16))
        emit_ln_dm(c, x_src, ln_dm, "b")
        with ExitStack() as st2:
            wdt = st2.enter_context(nsb("wdt_r", [128, 32, 128], BF16))
            p.dma("qpool", wdt[:], io["wdtT_t"], slot="wdt_r")
            wdtp = st2.enter_context(nsb("wdtp_r", [128, 32, 128], BF16))
            p.dma("qpool", wdtp[:], io["wdtpT_t"], slot="wdtp_r")
            wins = [st2.enter_context(nsb(f"wins{i}", [128, 16, 128], BF16))
                    for i in range(2)]
            wing = [st2.enter_context(nsb(f"wing{i}", [128, 16, 128], BF16))
                    for i in range(2)]
            hs0 = [st2.enter_context(nsb(f"hs0_{i}", [128, 2051], F32))
                   for i in range(2)]
            cv = [st2.enter_context(nsb(f"cv{i}", [128, 2048], F32))
                  for i in range(2)]
            hsb = [st2.enter_context(nsb(f"hsb{i}", [128, 2048], BF16))
                   for i in range(2)]
            sgb = [st2.enter_context(nsb(f"sgb{i}", [128, 2048], BF16))
                   for i in range(2)]
            memset(p, "dve", hs0[0][:, 0:3], 0.0)
            memset(p, "dve", hs0[1][:, 0:3], 0.0)
            p.dma("qpool", wins[0][:], io["winsT_t"][0], slot=wins[0].name)
            p.dma("qpool", wing[0][:], io["wingT_t"][0], slot=wing[0].name)
            for db in range(32):
                i = db % 2
                if db < 31:
                    ii = (db + 1) % 2
                    p.dma("qpool", wins[ii][:], io["winsT_t"][db + 1],
                          slot=wins[ii].name)
                    p.dma("qpool", wing[ii][:], io["wingT_t"][db + 1],
                          slot=wing[ii].name)
                for tn in range(4):
                    j = tn % 2
                    tsl = slice(tn * 512, (tn + 1) * 512)
                    pb, key = psB(c, j)
                    mm_group(p, pb, [(wins[i][:, kb, :], ln_dm[:, kb, tsl])
                                     for kb in range(16)],
                             okey=key, ikeys=[wins[i].name, f"ln_b.{tn}"])
                    act_copy(p, hs0[i][:, 3 + tn * 512: 3 + (tn + 1) * 512],
                             pb, okey=hs0[i].name, ikey=key)
                ts(p, cv[0][:], hs0[i][:, 0:2048],
                   c.convw_cols[:, db * 4:db * 4 + 1],
                   c.convb_cols[:, db:db + 1], OP.mult, OP.add,
                   okey="cv0", ikey=hs0[i].name)
                stt(p, cv[1][:], hs0[i][:, 1:2049],
                    c.convw_cols[:, db * 4 + 1:db * 4 + 2], cv[0][:],
                    OP.mult, OP.add, okey="cv1")
                stt(p, cv[0][:], hs0[i][:, 2:2050],
                    c.convw_cols[:, db * 4 + 2:db * 4 + 3], cv[1][:],
                    OP.mult, OP.add, okey="cv0")
                stt(p, cv[1][:], hs0[i][:, 3:2051],
                    c.convw_cols[:, db * 4 + 3:db * 4 + 4], cv[0][:],
                    OP.mult, OP.add, okey="cv1")
                act(p, hsb[i][:], cv[1][:], AF.Silu, okey=hsb[i].name, ikey="cv1")
                p.dma("qact", c.hs_s.ap()[db], hsb[i][:], slot="hs_s",
                      dst_key="hs_s", src_key=hsb[i].name)
                for tn in range(4):
                    j = tn % 2
                    tsl = slice(tn * 512, (tn + 1) * 512)
                    pb, key = psB(c, j)
                    mm_group(p, pb, [(wing[i][:, kb, :], ln_dm[:, kb, tsl])
                                     for kb in range(16)],
                             okey=key, ikeys=[wing[i].name, f"ln_b.{tn}"])
                    act(p, sgb[i][:, tsl], pb, AF.Silu, okey=sgb[i].name,
                        ikey=key)
                p.dma("qact", c.sg_s.ap()[db], sgb[i][:], slot="sg_s",
                      dst_key="sg_s", src_key=sgb[i].name)
                for tn in range(4):
                    pa, key = psA(c, tn)
                    mm(p, pa, wdt[:, db, :],
                       hsb[i][:, tn * 512:(tn + 1) * 512],
                       start=(db == 0), stop=(db == 31),
                       okey=key, lkey="wdt_r", rkey=hsb[i].name)
            tsbf = st2.enter_context(nsb("tsbf", [128, 2048], BF16))
            for tn in range(4):
                pa, key = psA(c, tn)
                act_copy(p, tsbf[:, tn * 512:(tn + 1) * 512], pa,
                         okey="tsbf", ikey=key)
            et = [st2.enter_context(nsb(f"et{i}", [128, 512], F32))
                  for i in range(2)]
            dtb = [st2.enter_context(nsb(f"dtb{i}", [128, 512], BF16))
                   for i in range(2)]
            for db in range(32):
                for tn in range(4):
                    j = (db * 4 + tn) % 2
                    pb, key = psB(c, j)
                    mm(p, pb, wdtp[:, db, :], tsbf[:, tn * 512:(tn + 1) * 512],
                       okey=key, lkey="wdtp_r", rkey="tsbf")
                    act(p, et[j][:], pb, AF.Exp, bias=c.bdtp_cols[:, db:db + 1],
                        okey=et[j].name, ikey=key)
                    act(p, dtb[j][:], et[j][:], AF.Ln, bias=c.c_one[:],
                        okey=dtb[j].name, ikey=et[j].name)
                    p.dma("qact", c.dt_s.ap()[db, :, tn * 512:(tn + 1) * 512],
                          dtb[j][:], slot="dt_s", dst_key="dt_s",
                          src_key=dtb[j].name)
            p.fence()
    nsb = c.nc.sbuf_tensor
    with ExitStack() as st2:
        wbc = st2.enter_context(nsb("wbc_r", [128, 32, 32], BF16))
        p.dma("qpool", wbc[:], io["wbcT_t"], slot="wbc_r")
        hsb2 = [st2.enter_context(nsb(f"hsc{i}", [128, 2048], BF16))
                for i in range(2)]
        p.dma("qsp", hsb2[0][:], c.hs_s.ap()[0], slot=hsb2[0].name)
        for db in range(32):
            i = db % 2
            if db < 31:
                ii = (db + 1) % 2
                p.dma("qsp", hsb2[ii][:], c.hs_s.ap()[db + 1],
                      slot=hsb2[ii].name)
            for tsl in range(16):
                mm(p, c.psB[:, tsl * 32:(tsl + 1) * 32],
                   hsb2[i][:, tsl * 128:(tsl + 1) * 128], wbc[:, db, :],
                   start=(db == 0), stop=(db == 31),
                   okey="psB0", lkey=hsb2[i].name, rkey="wbc_r")
        bc_sb = st2.enter_context(nsb("bc_sb", [128, 512], F32))
        act_copy(p, bc_sb[:], c.psB[:, 0:512], okey="bc_sb", ikey="psB0")
        bcT = st2.enter_context(nsb("bcT", [32, 16, 128], F32))
        for tsl in range(16):
            transpose(p, c.psB[0:32, 512:640], bc_sb[:, tsl * 32:(tsl + 1) * 32],
                      c.ident_f[:], okey="psB1", ikey="bc_sb")
            act_copy(p, bcT[:, tsl, :], c.psB[0:32, 512:640],
                     okey="bcT", ikey="psB1")
        p.dma("qact", c.bcflat_s.ap().rearrange("b n t -> (b n) t"),
              bcT[:].rearrange("p a b -> p (a b)"), slot="bcflat_s",
              dst_key="bcflat_s", src_key="bcT")
        p.fence()
    with ExitStack() as st2:
        B_rep = st2.enter_context(nsb("B_rep", [128, NS, 1024], BF16))
        C_rep = st2.enter_context(nsb("C_rep", [128, NS, 1024], BF16))
        dAv = st2.enter_context(nsb("dAv", [128, 2, 1024], F32))
        dBu = [st2.enter_context(nsb(f"dBu{i}", [128, NS, 1024], BF16))
               for i in range(2)]
        hv = st2.enter_context(nsb("hv", [128, 2, 1024], BF16))
        carry = st2.enter_context(nsb("carry", [128, 512], F32))
        dtb2 = [st2.enter_context(nsb(f"dtc{i}", [128, 1024], BF16))
                for i in range(2)]
        hsb3 = [st2.enter_context(nsb(f"hsd{i}", [128, 1024], BF16))
                for i in range(2)]
        sgb2 = [st2.enter_context(nsb(f"sgc{i}", [128, 1024], BF16))
                for i in range(2)]
        dtu = [st2.enter_context(nsb(f"dtu{i}", [128, 1024], BF16))
               for i in range(2)]
        t1 = st2.enter_context(nsb("t1_s", [128, 1024], F32))
        ybf = [st2.enter_context(nsb(f"ybf{i}", [128, 1024], BF16))
               for i in range(2)]
        memset(p, "dve", carry[:], 0.0)

        def stage_in(db, tch):
            i = db % 2
            tsl = slice(tch * 1024, (tch + 1) * 1024)
            p.dma("qsp", dtb2[i][:], c.dt_s.ap()[db, :, tsl],
                  slot=dtb2[i].name)
            p.dma("qsp", hsb3[i][:], c.hs_s.ap()[db, :, tsl],
                  slot=hsb3[i].name)
            p.dma("qsp", sgb2[i][:], c.sg_s.ap()[db, :, tsl],
                  slot=sgb2[i].name)
            tt(p, dtu[i][:], dtb2[i][:], hsb3[i][:], OP.mult,
               okey=dtu[i].name, akey=dtb2[i].name, bkey=hsb3[i].name)
            dtu_b = bass.AP(dtu[i], 0, [[1024, 128], [0, NS // 2], [1, 1024]])
            for half in range(2):
                hn = slice(half * 8, half * 8 + 8)
                p.op("pool", lambda e, o=dBu[i][:, hn, :],
                     a=B_rep[:, hn, :], bb=dtu_b:
                     e.tensor_tensor(o, a, bb, OP.mult),
                     [f"{dBu[i].name}.{half}"], [B_rep, dtu[i]])

        for tch in range(2):
            tsl = slice(tch * 1024, (tch + 1) * 1024)
            for bc, rep, rname in ((0, B_rep, "B_rep"), (1, C_rep, "C_rep")):
                srcap = bass.AP(c.bcflat_s, bc * NS * T + tch * 1024,
                                [[0, 128], [T, NS], [1, 1024]])
                p.dma("qpool", rep[:], srcap, slot=rname,
                      src_key="bcflat_s")
            stage_in(0, tch)
            for db in range(32):
                i = db % 2
                if db < 31:
                    stage_in(db + 1, tch)
                for n in range(NS):
                    s = n % 2
                    act(p, dAv[:, s, :], dtb2[i][:], AF.Exp,
                        scale=c.aneg[:, db * 16 + n:db * 16 + n + 1],
                        okey=f"dA{s}", ikey=dtb2[i].name)
                    scan(p, hv[:, s, :], dAv[:, s, :], dBu[i][:, n, :],
                         carry[:, db * 16 + n:db * 16 + n + 1],
                         okey=f"hv{s}",
                         keys=[f"dA{s}", f"{dBu[i].name}.{n // 8}", "carry"])
                    dve_copy(p, carry[:, db * 16 + n:db * 16 + n + 1],
                             hv[:, s, 1023:1024], okey="carry",
                             ikey=f"hv{s}")
                    tt(p, dBu[i][:, n, :], hv[:, s, :], C_rep[:, n, :],
                       OP.mult, okey=f"{dBu[i].name}.{n // 8}",
                       akey=f"hv{s}", bkey="C_rep")
                k0, k1 = f"{dBu[i].name}.0", f"{dBu[i].name}.1"
                p.op("dve", lambda e, o=dBu[i][:, 0:8, :], a=dBu[i][:, 0:8, :],
                     b=dBu[i][:, 8:16, :]: e.tensor_tensor(o, a, b, OP.add),
                     [k0], [k0, k1])
                p.op("dve", lambda e, o=dBu[i][:, 8:12, :], a=dBu[i][:, 0:4, :],
                     b=dBu[i][:, 4:8, :]: e.tensor_tensor(o, a, b, OP.add),
                     [k1], [k0])
                p.op("dve", lambda e, o=dBu[i][:, 12:14, :],
                     a=dBu[i][:, 8:10, :], b=dBu[i][:, 10:12, :]:
                     e.tensor_tensor(o, a, b, OP.add), [k1], [k1])
                ts(p, t1[:], hsb3[i][:], c.d_cols[:, db:db + 1], None,
                   OP.mult, okey="t1_s", ikey=hsb3[i].name)
                tt(p, t1[:], t1[:], dBu[i][:, 12, :], OP.add,
                   okey="t1_s", akey="t1_s", bkey=f"{dBu[i].name}.1")
                tt(p, t1[:], t1[:], dBu[i][:, 13, :], OP.add,
                   okey="t1_s", akey="t1_s", bkey=f"{dBu[i].name}.1")
                tt(p, ybf[i][:], t1[:], sgb2[i][:], OP.mult,
                   okey=ybf[i].name, akey="t1_s", bkey=sgb2[i].name)
                p.dma("qact", c.y_s.ap()[db, :, tsl], ybf[i][:],
                      slot="y_s", dst_key="y_s", src_key=ybf[i].name)
        p.fence()
    with ExitStack() as st:
        nsb = c.nc.sbuf_tensor
        woutb = st.enter_context(nsb("woutb_r", [128, 32, 2048], BF16))
        for i in range(8):
            p.dma("qpool", woutb[:, i * 4:(i + 1) * 4, :],
                  c.io["woutbT_t"][:, i * 4:(i + 1) * 4, :], slot="woutb_r",
                  dst_key="woutb_r")
        yb = st.enter_context(nsb("yb", [128, 32, 512], BF16))
        xt = st.enter_context(nsb("xtb", [128, 2048], F32))
        xnew = st.enter_context(nsb("xnb", [128, 2048], F32))
        for tch in range(4):
            tsl = slice(tch * 512, (tch + 1) * 512)
            for db in range(32):
                p.dma("qsp", yb[:, db, :], c.y_s.ap()[db, :, tsl],
                      slot="yb", dst_key="yb", src_key="y_s")
            for tq in range(4):
                row = tch * 512 + tq * 128
                p.dma("qsp", xt[:], x_src[row:row + 128, :], slot="xtb",
                      src_key=f"{x_src.tensor.name}.{row // 512}")
                for oc in range(4):
                    pb, key = psB(c, oc % 2)
                    mm_group(p, pb, [(yb[:, db, tq * 128:(tq + 1) * 128],
                                      woutb[:, db, oc * 512:(oc + 1) * 512])
                                     for db in range(32)],
                             okey=key, ikeys=["yb", "woutb_r"])
                    osl = slice(oc * 512, (oc + 1) * 512)
                    tt(p, xnew[:, osl], pb, xt[:, osl], OP.add,
                       okey="xnb", akey=key, bkey="xtb")
                p.dma("qact", x_dst[row:row + 128, :], xnew[:],
                      slot=f"{x_dst_name}.{row // 512}",
                      dst_key=f"{x_dst_name}.{row // 512}", src_key="xnb")
        p.fence()


def build_program(phases=ALL_PHASES, x_out_taps=()):
    nc = bass.Bass("TRN2", target_bir_lowering=False)
    shared_names = list(stage_shared_dummy().keys())
    io = {}
    for name, arr in stage_shared_dummy().items():
        io[name] = nc.dram_tensor(name, list(arr.shape),
                                  mybir.dt.from_np(arr.dtype),
                                  kind="ExternalInput").ap()
    x0 = nc.dram_tensor("x0", [T, D], F32, kind="ExternalInput").ap()
    out = nc.dram_tensor("out", [T, D], F32, kind="ExternalOutput").ap()
    with ExitStack() as stack:
        p = Prog(nc)
        stack.callback(p.close)
        c = C(nc, p, stack, io)
        build_init(c)
        xA, xB = c.xA.ap(), c.xB.ap()
        if "w2p" in phases and "attn_a" not in phases:
            build_w2p(c)
        cur, cur_name = x0, "x0"
        if "attn_a" in phases:
            build_attn_a(c, cur, xA, "xA", with_w2p="w2p" in phases)
            cur, cur_name = xA, "xA"
        if "ffn1" in phases:
            dst, dname = (xB, "xB") if ("attn_b" in phases or "ffn2" in phases) \
                else (out, "out")
            build_ffn(c, cur, dst, dname, "1")
            cur, cur_name = dst, dname
        if "attn_b" in phases:
            dst, dname = (xA, "xA") if "ffn2" in phases else (out, "out")
            build_attn_b(c, cur, dst, dname)
            cur, cur_name = dst, dname
        if "ffn2" in phases:
            build_ffn(c, cur, out, "out", "2")
            cur, cur_name = out, "out"
        if cur_name != "out":
            p._read_deps("sp", [f"{cur_name}.{k}" for k in range(4)])
            p.dma("qsp", out, cur, slot="out", dst_key="out", src_key=cur_name)
        p.barrier_wait_all_dma()
        p.emit()
    nc.finalize()
    return nc


_DUMMY = None


def stage_shared_dummy():
    global _DUMMY
    if _DUMMY is None:
        w = {}
        rng = np.random.default_rng(0)
        w["L0_attn_a_w_qkv"] = np.zeros((6144, 2048), np.float32)
        w["L0_attn_a_b_qkv"] = np.zeros((6144,), np.float32)
        w["L0_attn_a_w_out"] = np.zeros((2048, 2048), np.float32)
        w["L0_attn_a_b_out"] = np.zeros((2048,), np.float32)
        w["L0_ffn_W1"] = np.zeros((8192, 2048), np.float32)
        w["L0_ffn_W2"] = np.zeros((2048, 8192), np.float32)
        w["L0_ffn_W3"] = np.zeros((8192, 2048), np.float32)
        w["L0_attn_b_W_in_states"] = np.zeros((4096, 2048), np.float32)
        w["L0_attn_b_W_in_gates"] = np.zeros((4096, 2048), np.float32)
        w["L0_attn_b_conv_w"] = np.zeros((4096, 4), np.float32)
        w["L0_attn_b_conv_b"] = np.zeros((4096,), np.float32)
        w["L0_attn_b_W_dt"] = np.zeros((128, 4096), np.float32)
        w["L0_attn_b_W_b"] = np.zeros((16, 4096), np.float32)
        w["L0_attn_b_W_c"] = np.zeros((16, 4096), np.float32)
        w["L0_attn_b_W_dtproj"] = np.zeros((4096, 128), np.float32)
        w["L0_attn_b_b_dtproj"] = np.zeros((4096,), np.float32)
        w["L0_attn_b_A_log"] = np.zeros((4096, 16), np.float32)
        w["L0_attn_b_D"] = np.zeros((4096,), np.float32)
        w["L0_attn_b_W_out"] = np.zeros((2048, 4096), np.float32)
        _DUMMY = stage_shared(w)
    return _DUMMY


def kernel(**inputs):
    x = np.asarray(inputs["L0_attn_a_x"])
    staged = stage_shared(inputs)
    nc = build_program(ALL_PHASES)
    in_maps = []
    for b in range(8):
        m = dict(staged)
        m["x0"] = np.ascontiguousarray(x[b], dtype=np.float32)
        in_maps.append(m)
    res = run_bass_kernel_spmd(nc, in_maps, core_ids=list(range(8)))
    out = np.stack([np.asarray(res.results[b]["out"]) for b in range(8)])
    return out.astype(np.float32, copy=False)
